# version 02
# baseline (speedup 1.0000x reference)
import contextlib
import numpy as np
import ml_dtypes
import concourse.bass as bass
import concourse.mybir as mybir
from concourse.bass_core_rust import IndirectOffsetOnAxis
from concourse.bass_utils import run_bass_kernel_spmd

F32 = mybir.dt.float32
BF16 = mybir.dt.bfloat16
I32 = mybir.dt.int32
AF = mybir.ActivationFunctionType
ALU = mybir.AluOpType
AX = mybir.AxisListType

T = 4096
D = 1024
G = D // 128
NCORE = 8
TB = 512
NTB = T // TB

ENGINES = ("sync", "tensor", "vector", "scalar", "gpsimd")


class P:

    def __init__(self, nc, ctx):
        self.nc = nc
        self.ctx = ctx
        self.q = {e: [] for e in ENGINES}
        self.sems = {}
        self.tot = {}
        self.lastw = {}
        self.dqueue = {}

    def _sem(self, name):
        if name not in self.sems:
            self.sems[name] = self.ctx.enter_context(self.nc.semaphore(name))
            self.tot[name] = 0
        return self.sems[name]

    def mark(self, sem):
        self._sem(sem)
        return self.tot[sem]

    OWN = {"vector": "ve", "scalar": "se", "gpsimd": "ge"}

    def op(self, eng, f, waits=(), inc=None, inc_n=1, is_dma=False):
        wl = []
        own = None if is_dma else self.OWN.get(eng)
        if own is not None:
            self._sem(own)
            if self.tot[own] > 0:
                wl.append((own, self.tot[own]))
                self.lastw[(eng, own)] = self.tot[own]
            inc, inc_n = own, 1
        for w in waits:
            if isinstance(w, str):
                self._sem(w)
                w = (w, self.tot[w])
            name, val = w
            self._sem(name)
            assert val <= self.tot[name], f"wait {name}>={val} > built {self.tot[name]}"
            if val > 0 and self.lastw.get((eng, name), 0) < val:
                wl.append((name, val))
                self.lastw[(eng, name)] = val
        if inc is not None:
            self._sem(inc)
            self.tot[inc] += inc_n

        def run(e, wl=tuple(wl), f=f, inc=inc, inc_n=inc_n):
            for s_, v in wl:
                e.wait_ge(self.sems[s_], v)
            ins = f(e)
            if inc is not None:
                ins.then_inc(self.sems[inc], inc_n)
        self.q[eng].append(run)
        return self.tot[inc] if inc is not None else None

    def wait_only(self, eng, waits):
        wl = []
        for w in waits:
            if isinstance(w, str):
                self._sem(w)
                w = (w, self.tot[w])
            wl.append(w)

        def run(e, wl=tuple(wl)):
            for s_, v in wl:
                e.wait_ge(self.sems[s_], v)
        self.q[eng].append(run)

    def dma(self, qname, out, in_, waits=(), **kw):
        return self.idma(qname, lambda e: e.dma_start(out=out, in_=in_, **kw),
                         waits)

    def idma(self, qname, f, waits=()):
        dq = self.dqueue.setdefault(qname, {"cnt": 0, "pend": {0: None, 1: None}})
        par = dq["cnt"] % 2
        sem = f"D{qname}{par}"
        if dq["pend"][par] is not None:
            self.wait_only(qname, [(sem, dq["pend"][par])])
        self.op(qname, f, waits, sem, 16, is_dma=True)
        tok = (sem, self.tot[sem])
        dq["pend"][par] = tok[1]
        dq["cnt"] += 1
        return tok

    def dma_flush(self, qname):
        dq = self.dqueue.get(qname)
        if not dq:
            return
        for par in (0, 1):
            if dq["pend"][par] is not None:
                self.wait_only(qname, [(f"D{qname}{par}", dq["pend"][par])])
                dq["pend"][par] = None

    def finish(self):
        blk = self.ctx.enter_context(self.nc.Block())
        for e in ENGINES:
            if self.q[e]:
                def runner(eng, lst=tuple(self.q[e])):
                    for r in lst:
                        r(eng)
                getattr(blk, e)(runner)

    def sbuf(self, name, shape, dt=F32):
        return self.ctx.enter_context(self.nc.sbuf_tensor(name, list(shape), dt))

    def psum(self, name, shape, dt=F32):
        return self.ctx.enter_context(self.nc.psum_tensor(name, list(shape), dt))


def fm_src(ap, cols=None):
    a = ap if cols is None else ap[:, cols]
    return a.rearrange("(g p) t -> p g t", p=128)


def load_ident(p):
    ident = p.sbuf("ident", [128, 128], BF16)
    idc = p.nc.inline_tensor(np.eye(128, dtype=ml_dtypes.bfloat16), "identc").ap()
    p.ident_tok = p.dma("sync", ident[:], idc)
    return ident


def load_wT(p, name, w_ap, R, C, ident, stage=None, pt=None):
    RG, CG = R // 128, C // 128
    w = stage if stage is not None else p.sbuf(name + "_w", [128, RG, C], BF16)
    wT = p.sbuf(name + "_wT", [128, CG, R], BF16)
    if pt is None:
        pt = p.psum(name + "_pt", [128, 128], BF16)
    ld = p.dma("gpsimd", w[:, 0:RG, 0:C], w_ap.rearrange("(g p) c -> p g c", p=128),
               waits=["te"])
    for rg in range(RG):
        for cg in range(CG):
            tm = p.op("tensor",
                      lambda e, rg=rg, cg=cg, pt=pt, w=w: e.transpose(
                          pt[:, :], w[:, rg, cg * 128:(cg + 1) * 128], ident[:]),
                      waits=[ld, p.ident_tok, "ve"], inc="te")
            p.op("vector",
                 lambda e, rg=rg, cg=cg, pt=pt: e.tensor_copy(
                     wT[:, cg, rg * 128:(rg + 1) * 128], pt[:, :]),
                 waits=[("te", tm)], inc="ve")
    return wT


def proj_fm(p, wT, x_t, W, OGS, KG, banks, evac, first_waits=()):
    for i, og in enumerate(OGS):
        bank = banks[i % len(banks)]
        fw = list(first_waits) + ["se", "ve"]
        mm = None
        for kg in range(KG):
            mm = p.op("tensor",
                      lambda e, og=og, kg=kg, bank=bank: e.matmul(
                          bank[:, 0:W], wT[:, kg, og * 128:(og + 1) * 128],
                          x_t[:, kg, 0:W],
                          start=(kg == 0), stop=(kg == KG - 1)),
                      waits=fw if kg == 0 else [], inc="te")
        evac(og, bank, mm)


def emit_ln_fm(p, xt, yt, cm, W, eps=1e-5):
    ve0 = p.mark("ve")
    if cm.get("slim"):
        oc = cm["onesch"] if xt.dtype == BF16 else cm["onesc"]
        for g in range(G):
            sv = p.op("scalar", lambda e, g=g: e.activation(
                cm["sq"][:, 0:W], xt[:, g, :W], AF.Square),
                waits=[("ve", ve0), "te"], inc="se")
            p.op("tensor", lambda e, g=g, oc=oc: e.matmul(
                cm["prow"][0:1, 0:W], oc[:, 0:1], xt[:, g, :W],
                start=(g == 0), stop=(g == G - 1)), waits=[("se", sv)], inc="te")
            p.op("tensor", lambda e, g=g: e.matmul(
                cm["prow"][0:1, W:2 * W], cm["onesc"][:, 0:1], cm["sq"][:, 0:W],
                start=(g == 0), stop=(g == G - 1)), waits=[], inc="te")
        tv = p.mark("te")
    else:
        for g in range(G):
            p.op("scalar", lambda e, g=g: e.activation(
                cm["sq"][:, g * W:(g + 1) * W], xt[:, g, :W], AF.Square),
                waits=[("ve", ve0)], inc="se")
        sqv = p.mark("se")
        for g in range(G):
            p.op("tensor", lambda e, g=g: e.matmul(
                cm["prow"][0:1, 0:W], cm["onesc"][:, 0:1], xt[:, g, :W],
                start=(g == 0), stop=(g == G - 1)), waits=[("se", sqv), "ve"], inc="te")
        for g in range(G):
            p.op("tensor", lambda e, g=g: e.matmul(
                cm["prow"][0:1, W:2 * W], cm["onesc"][:, 0:1], cm["sq"][:, g * W:(g + 1) * W],
                start=(g == 0), stop=(g == G - 1)), waits=[], inc="te")
        tv = p.mark("te")
    r0, r1, rb = cm["r0"], cm["r1"], cm["rowbuf"]
    p.op("vector", lambda e: e.tensor_scalar_mul(r0[0:1, 0:W], cm["prow"][0:1, 0:W], 1.0 / D),
         waits=[("te", tv)], inc="ve")
    p.op("vector", lambda e: e.tensor_scalar_mul(r1[0:1, 0:W], cm["prow"][0:1, W:2 * W], 1.0 / D),
         waits=[], inc="ve")
    p.op("vector", lambda e: e.tensor_tensor(cm["sq"][0:1, 0:W], r0[0:1, 0:W], r0[0:1, 0:W], ALU.mult),
         waits=[], inc="ve")
    p.op("vector", lambda e: e.tensor_sub(r1[0:1, 0:W], r1[0:1, 0:W], cm["sq"][0:1, 0:W]),
         waits=[], inc="ve")
    p.op("vector", lambda e: e.tensor_scalar_add(r1[0:1, 0:W], r1[0:1, 0:W], float(eps)),
         waits=[], inc="ve")
    p.op("scalar", lambda e: e.activation(r1[0:1, 0:W], r1[0:1, 0:W], AF.Sqrt),
         waits=["ve"], inc="se")
    p.op("vector", lambda e: e.reciprocal(rb[0:1, 0:W], r1[0:1, 0:W]),
         waits=["se"], inc="ve")
    p.op("vector", lambda e: e.tensor_mul(rb[0:1, W:2 * W], r0[0:1, 0:W], rb[0:1, 0:W]),
         waits=[], inc="ve")
    p.op("vector", lambda e: e.tensor_scalar_mul(rb[0:1, W:2 * W], rb[0:1, W:2 * W], -1.0),
         waits=[], inc="ve")
    vv = p.mark("ve")
    p.op("tensor", lambda e: e.matmul(cm["pbc"][:, 0:W], cm["ones_row"][0:1, :],
                                      rb[0:1, 0:W], start=True, stop=True),
         waits=[("ve", vv)], inc="te")
    p.op("tensor", lambda e: e.matmul(cm["pbc"][:, W:2 * W], cm["ones_row"][0:1, :],
                                      rb[0:1, W:2 * W], start=True, stop=True),
         waits=[], inc="te")
    tv2 = p.mark("te")
    if cm.get("slim"):
        for g in range(G):
            p.op("vector", lambda e, g=g: e.tensor_mul(yt[:, g, :W], xt[:, g, :W],
                                                       cm["pbc"][:, 0:W]),
                 waits=[("te", tv2)], inc="ve")
            p.op("vector", lambda e, g=g: e.tensor_add(yt[:, g, :W], yt[:, g, :W],
                                                       cm["pbc"][:, W:2 * W]),
                 waits=[], inc="ve")
        return p.mark("ve")
    p.op("vector", lambda e: e.tensor_copy(cm["bcs"][:, 0:2 * W], cm["pbc"][:, 0:2 * W]),
         waits=[("te", tv2)], inc="ve")
    vv2 = p.mark("ve")
    for g in range(G):
        p.op("vector", lambda e, g=g: e.tensor_mul(yt[:, g, :W], xt[:, g, :W], cm["bcs"][:, 0:W]),
             waits=[("ve", vv2)], inc="ve")
        p.op("vector", lambda e, g=g: e.tensor_add(yt[:, g, :W], yt[:, g, :W], cm["bcs"][:, W:2 * W]),
             waits=[], inc="ve")
    return p.mark("ve")


def make_common(p, W=TB, slim=False):
    if slim:
        cm = {
            "onesc": p.sbuf("onesc", [128, 1], F32),
            "onesch": p.sbuf("onesch", [128, 1], BF16),
            "ones_row2": None,
            "prow": getattr(p, "shared_prow", None) or p.psum("prow", [1, 2 * W], F32),
            "rowbuf": getattr(p, "shared_rowbuf", None) or p.sbuf("rowbuf", [1, 2 * W], F32),
            "pbc": getattr(p, "shared_pbc", None) or p.psum("pbc", [128, 2 * W], F32),
            "bcs": None,
            "r0": getattr(p, "shared_r0", None) or p.sbuf("r0", [1, W], F32),
            "r1": getattr(p, "shared_r1", None) or p.sbuf("r1", [1, W], F32),
            "sq": getattr(p, "shared_sq", None) or p.sbuf("sq1", [128, W], F32),
            "ones_row": p.ones_row,
            "slim": True,
        }
        p.op("vector", lambda e: e.memset(cm["onesc"][:], 1.0))
        p.op("vector", lambda e: e.memset(cm["onesch"][:], 1.0))
        return cm
    cm = {
        "onesc": p.sbuf("onesc", [128, 1], F32),
        "ones_row": p.sbuf("ones_row", [1, 128], F32),
        "prow": p.psum("prow", [1, 2 * W], F32),
        "rowbuf": p.sbuf("rowbuf", [1, 2 * W], F32),
        "pbc": p.psum("pbc", [128, 2 * W], F32),
        "bcs": p.sbuf("bcs", [128, 2 * W], F32),
        "r0": p.sbuf("r0", [1, W], F32),
        "r1": p.sbuf("r1", [1, W], F32),
        "sq": p.sbuf("sq", [128, G * W], F32),
    }
    p.op("vector", lambda e: e.memset(cm["onesc"][:], 1.0), inc="ve")
    p.op("vector", lambda e: e.memset(cm["ones_row"][:], 1.0), inc="ve")
    return cm


def build_l2_ffn(debug=False):
    nc = bass.Bass("TRN2", target_bir_lowering=False)
    x_in = nc.dram_tensor("x_in", [D, T], F32, kind="ExternalInput").ap()
    w_r = nc.dram_tensor("w_r", [D, D], F32, kind="ExternalInput").ap()
    w_k = nc.dram_tensor("w_k", [D, D], F32, kind="ExternalInput").ap()
    w_v = nc.dram_tensor("w_v", [D, D], F32, kind="ExternalInput").ap()
    mu_r = nc.dram_tensor("mu_r", [D, 1], F32, kind="ExternalInput").ap()
    mu_k = nc.dram_tensor("mu_k", [D, 1], F32, kind="ExternalInput").ap()
    x_out = nc.dram_tensor("x_out", [D, T], F32, kind="ExternalOutput").ap()
    if debug:
        d_xr = nc.dram_tensor("d_xr", [D, TB], F32, kind="ExternalOutput").ap()
        d_rr = nc.dram_tensor("d_rr", [D, TB], F32, kind="ExternalOutput").ap()
        d_ka = nc.dram_tensor("d_ka", [D, TB], F32, kind="ExternalOutput").ap()
        d_yy = nc.dram_tensor("d_yy", [D, TB], F32, kind="ExternalOutput").ap()
        d_wt = nc.dram_tensor("d_wt", [128, D], F32, kind="ExternalOutput").ap()

    locals_dbg = {"d_xr": d_xr, "d_rr": d_rr, "d_ka": d_ka, "d_yy": d_yy} if debug else {}
    with contextlib.ExitStack() as ctx:
        p = P(nc, ctx)
        ident = load_ident(p)
        wstage = p.sbuf("wstage", [128, G, D], BF16)
        wpt = p.psum("wpt", [128, 128], BF16)
        wTr = load_wT(p, "wr", w_r, D, D, ident, wstage, wpt)
        wTk = load_wT(p, "wk", w_k, D, D, ident, wstage, wpt)
        wTv = load_wT(p, "wv", w_v, D, D, ident, wstage, wpt)

        smu = p.sbuf("smu", [128, 4, G], F32)
        t1 = p.dma("sync", smu[:, 0, :], mu_r.rearrange("(g p) c -> p (g c)", p=128),
                   allow_slow_non_contiguous=True)
        t2 = p.dma("sync", smu[:, 2, :], mu_k.rearrange("(g p) c -> p (g c)", p=128),
                   allow_slow_non_contiguous=True)
        p.op("vector", lambda e: e.tensor_scalar(
            smu[:, 1, :], smu[:, 0, :], -1.0, 1.0, ALU.mult, ALU.add),
            waits=[t1, t2], inc="ve")
        p.op("vector", lambda e: e.tensor_scalar(
            smu[:, 3, :], smu[:, 2, :], -1.0, 1.0, ALU.mult, ALU.add),
            waits=[], inc="ve")

        xb = [p.sbuf(f"xb{i}", [128, G, TB + 1], F32) for i in range(2)]
        xr = p.sbuf("xr", [128, G, TB], BF16)
        xk = p.sbuf("xk", [128, G, TB], BF16)
        rr = p.sbuf("rr", [128, G, TB], F32)
        ka = p.sbuf("ka", [128, G, TB], BF16)
        yy = p.sbuf("yy", [128, G, TB], F32)
        yo = yy
        pmm = [p.psum(f"pmm{i}", [128, TB], F32) for i in range(2)]
        tmp = p.sbuf("tmp", [128, TB], F32)
        cm = make_common(p)

        blk_done = [0, 0]
        store_tok = None
        for ib in range(NTB):
            cur = xb[ib % 2]
            t0 = ib * TB
            if store_tok is not None:
                p.wait_only("vector", [store_tok])
            ldw = [("ve", blk_done[ib % 2])] if blk_done[ib % 2] else []
            if ib == 0:
                mz = p.op("vector", lambda e, cur=cur: e.memset(cur[:, :, 0:1].rearrange("p g c -> p (g c)"), 0.0), inc="ve")
                xld = p.dma("sync", cur[:, :, 1:TB + 1], fm_src(x_in, slice(0, TB)),
                            waits=[("ve", mz)])
            else:
                xld = p.dma("sync", cur[:, :, 0:TB + 1], fm_src(x_in, slice(t0 - 1, t0 + TB)),
                            waits=ldw)
            for g in range(G):
                p.op("vector", lambda e, g=g, cur=cur: e.tensor_scalar_mul(
                    tmp[:, :], cur[:, g, 1:TB + 1], smu[:, 0, g:g + 1]),
                    waits=[xld], inc="ve")
                p.op("vector", lambda e, g=g, cur=cur: e.scalar_tensor_tensor(
                    xr[:, g, :], cur[:, g, 0:TB], smu[:, 1, g:g + 1], tmp[:, :],
                    ALU.mult, ALU.add), waits=[], inc="ve")
                p.op("vector", lambda e, g=g, cur=cur: e.tensor_scalar_mul(
                    tmp[:, :], cur[:, g, 1:TB + 1], smu[:, 2, g:g + 1]),
                    waits=[], inc="ve")
                p.op("vector", lambda e, g=g, cur=cur: e.scalar_tensor_tensor(
                    xk[:, g, :], cur[:, g, 0:TB], smu[:, 3, g:g + 1], tmp[:, :],
                    ALU.mult, ALU.add), waits=[], inc="ve")
            def ev_r(og, bank, mm):
                p.op("scalar", lambda e, og=og, bank=bank: e.activation(
                    rr[:, og, :], bank[:, 0:TB], AF.Sigmoid),
                    waits=[("te", mm)], inc="se")
            proj_fm(p, wTr, xr, TB, range(G), G, pmm, ev_r)
            def ev_k(og, bank, mm):
                p.op("vector", lambda e, og=og, bank=bank: e.tensor_scalar_max(
                    cm["sq"][:, og * TB:(og + 1) * TB], bank[:, 0:TB], 0.0),
                    waits=[("te", mm)], inc="ve")
                p.op("scalar", lambda e, og=og: e.activation(
                    ka[:, og, :], cm["sq"][:, og * TB:(og + 1) * TB], AF.Square),
                    waits=["ve"], inc="se")
            proj_fm(p, wTk, xk, TB, range(G), G, pmm, ev_k)
            def ev_v(og, bank, mm):
                p.op("vector", lambda e, og=og, bank=bank: e.tensor_tensor(
                    yy[:, og, :], bank[:, 0:TB], rr[:, og, :], ALU.mult),
                    waits=[("te", mm)], inc="ve")
                p.op("vector", lambda e, og=og, cur=cur: e.tensor_add(
                    yy[:, og, :], yy[:, og, :], cur[:, og, 1:TB + 1]),
                    waits=[], inc="ve")
            proj_fm(p, wTv, ka, TB, range(G), G, pmm, ev_v)
            blk_done[ib % 2] = p.mark("ve")
            if debug and ib == 0:
                dbf = p.sbuf("dbf", [128, G, TB], F32)
                dbf_tok = None
                for nm, t_ in (("d_xr", xr), ("d_rr", rr), ("d_ka", ka), ("d_yy", yy)):
                    if t_.dtype != F32:
                        for g in range(G):
                            p.op("vector", lambda e, g=g, t_=t_: e.tensor_copy(
                                dbf[:, g, :], t_[:, g, :]),
                                waits=(["ve", "se"] + ([dbf_tok] if dbf_tok else [])))
                        srcb = dbf
                    else:
                        srcb = t_
                    tk = p.dma("sync", fm_src(locals_dbg[nm]), srcb[:],
                               waits=["ve", "se"])
                    if srcb is dbf:
                        dbf_tok = tk
                    if t_ is yy:
                        p.wait_only("vector", [tk])
                        p.wait_only("scalar", [tk])
                p.dma("gpsimd", d_wt.rearrange("p (g c) -> p g c", g=1), wTr[:, 0:1, 0:D],
                      waits=["ve"])
            emit_ln_fm(p, yy, yo, cm, TB)
            store_tok = p.dma("sync", fm_src(x_out, slice(t0, t0 + TB)), yo[:],
                              waits=["ve"])
        for qn in list(p.dqueue):
            p.dma_flush(qn)
        p.finish()
    return nc


def run_stage(nc, in_maps, outs=("x_out",)):
    res = run_bass_kernel_spmd(nc, in_maps, core_ids=list(range(len(in_maps))))
    return res


def kernel(**inputs):
    ii = inputs
    x0 = ii["L0_attn_x"]
    xfm = [np.ascontiguousarray(x0[c].T) for c in range(NCORE)]

    def stage(builder, maps, out="x_out"):
        res = run_stage(builder(), maps)
        return [res.results[c][out] for c in range(NCORE)]

    xfm = stage(build_l0_attn, [{
        "x_in": xfm[c], "wq": ii["L0_attn_Wq"], "wk": ii["L0_attn_Wk"],
        "wv": ii["L0_attn_Wv"], "wo": ii["L0_attn_Wo"]} for c in range(NCORE)])
    lg = stage(build_l0_logits, [{
        "x_in": xfm[c], "rw": ii["L0_ffn_router_w"]} for c in range(NCORE)],
        out="lg_out")
    lgall = np.ascontiguousarray(np.concatenate(lg, axis=0))
    xfm_all = np.ascontiguousarray(np.concatenate(xfm, axis=1))
    res = run_stage(build_l0_moe(), [{
        "lgall": lgall, "xfm": xfm_all,
        "eid": np.full((128, 1), float(c), np.float32),
        "wg_e": ii["L0_ffn_wg"][c], "wu_e": ii["L0_ffn_wu"][c],
        "wd_e": ii["L0_ffn_wd"][c]} for c in range(NCORE)])
    eo = [res.results[c]["eo_e"] for c in range(NCORE)]
    xfm = stage(build_l0_moefin, [dict(
        {"x1f": xfm[c]},
        **{f"eo{e}": np.ascontiguousarray(eo[e][:, c * T:(c + 1) * T])
           for e in range(MOE_E)}) for c in range(NCORE)])
    xfm = stage(build_l1_attn, [{
        "x_in": xfm[c], "nw": ii["L1_attn_norm_w"].reshape(D, 1),
        "w_in": ii["L1_attn_in_proj_w"], "aw": ii["L1_attn_A_proj_w"],
        "ab_": ii["L1_attn_A_proj_b"].reshape(2048, 1),
        "abase": ii["L1_attn_A_base"].reshape(16, 1),
        "bw": ii["L1_attn_B_proj_w"], "cw": ii["L1_attn_C_proj_w"],
        "dvec": ii["L1_attn_D"].reshape(2048, 1),
        "ow": ii["L1_attn_out_proj_w"]} for c in range(NCORE)])
    xfm = stage(build_l1_ffn, [{
        "x_in": xfm[c], "w1": ii["L1_ffn_W1"],
        "b1": ii["L1_ffn_b1"].reshape(4096, 1), "w2": ii["L1_ffn_W2"],
        "b2": ii["L1_ffn_b2"].reshape(D, 1)} for c in range(NCORE)])
    xfm = stage(build_l2_attn, [{
        "x_in": xfm[c], "wq": ii["L2_attn_Wq"], "bq": ii["L2_attn_bq"].reshape(D, 1),
        "wk": ii["L2_attn_Wk"], "bk": ii["L2_attn_bk"].reshape(D, 1),
        "wv": ii["L2_attn_Wv"], "bv": ii["L2_attn_bv"].reshape(1, D),
        "wp": ii["L2_attn_Wp"], "bp": ii["L2_attn_bp"].reshape(D, 1)}
        for c in range(NCORE)])
    xfm = stage(build_l2_ffn, [{
        "x_in": xfm[c], "w_r": ii["L2_ffn_W_r"], "w_k": ii["L2_ffn_W_k"],
        "w_v": ii["L2_ffn_W_v"], "mu_r": ii["L2_ffn_mu_r"].reshape(D, 1),
        "mu_k": ii["L2_ffn_mu_k"].reshape(D, 1)} for c in range(NCORE)])
    l3a = {"wq": ii["L3_attn_Wq"], "wk": ii["L3_attn_Wk"],
           "wv": ii["L3_attn_Wv"], "wo": ii["L3_attn_Wo"]}
    for nm in ("UL", "VR", "UR", "VL"):
        for l in range(4):
            l3a[f"{nm}{l}"] = ii[f"L3_attn_{nm}{l}"]
    xfm = stage(build_l3_attn, [dict({"x_in": xfm[c]}, **l3a)
                                for c in range(NCORE)])
    xfm = stage(build_l3_ffn, [{
        "x_in": xfm[c], "wqkv": ii["L3_ffn_Wqkv"], "qsc": ii["L3_ffn_q_scale"],
        "ksc": ii["L3_ffn_k_scale"], "wo6": ii["L3_ffn_Wo"],
        "lng": ii["L3_ffn_ln_g"], "lnb": ii["L3_ffn_ln_b"],
        "w16": ii["L3_ffn_W1"], "b16": ii["L3_ffn_b1"],
        "w26": ii["L3_ffn_W2"], "b26": ii["L3_ffn_b2"]} for c in range(NCORE)])
    return np.ascontiguousarray(
        np.stack([xfm[c].T for c in range(NCORE)]).astype(np.float32))


def wload_quant_T(p, name, w_ap, R, C, ident, stage, wpt, CW=2048, shared=None):
    RG, CG = R // 128, C // 128
    NCC = (C + CW - 1) // CW
    wT = p.sbuf(name + "_wT", [128, CG, R], BF16)
    if shared is None:
        shared = {}
        shared["acc"] = p.sbuf(name + "_acc", [128, 1], F32)
        shared["gam"] = p.sbuf(name + "_gam", [128, 1], F32)
        shared["pr"] = p.psum(name + "_pr", [1, 1], F32)
        shared["pc"] = p.psum(name + "_pc", [128, 1], F32)
        shared["on"] = p.sbuf(name + "_on", [128, 1], F32)
        p.op("vector", lambda e, s=shared: e.memset(s["on"][:], 1.0))
    acc, gam, prow, pcol, onesc = (shared["acc"], shared["gam"], shared["pr"],
                                   shared["pc"], shared["on"])
    p.op("vector", lambda e: e.memset(acc[:], 0.0))
    st0 = stage[:, 0:CW]
    st1 = stage[:, CW:2 * CW]
    for rg in range(RG):
        for cc in range(NCC):
            c0, c1 = cc * CW, min((cc + 1) * CW, C)
            tk = p.dma("sync", st0[:, 0:c1 - c0], w_ap[rg * 128:(rg + 1) * 128, c0:c1],
                       waits=["ve", "se"])
            p.op("scalar", lambda e, c=c1 - c0: e.activation(st1[:, 0:c], st0[:, 0:c], AF.Abs),
                 waits=[tk])
            p.op("vector", lambda e, c=c1 - c0: e.reduce_sum(gam[:], st1[:, 0:c], axis=AX.X),
                 waits=["se"])
            p.op("vector", lambda e: e.tensor_add(acc[:], acc[:], gam[:]))
    sv = p.op("tensor", lambda e: e.matmul(prow[0:1, :], onesc[:, 0:1], acc[:, 0:1],
                                           start=True, stop=True), waits=["ve"], inc="te")
    p.op("vector", lambda e: e.tensor_scalar_mul(acc[0:1, 0:1], prow[0:1, 0:1],
                                                 1.0 / (R * C)), waits=[("te", sv)])
    p.op("vector", lambda e: e.tensor_scalar_add(acc[0:1, 0:1], acc[0:1, 0:1], 1e-5))
    p.op("vector", lambda e: e.reciprocal(acc[0:1, 0:1], acc[0:1, 0:1]))
    bv = p.op("tensor", lambda e: e.matmul(pcol[:, 0:1], p.ones_row[0:1, :],
                                           acc[0:1, 0:1], start=True, stop=True),
              waits=["ve"], inc="te")
    p.op("vector", lambda e: e.tensor_copy(gam[:], pcol[:, 0:1]), waits=[("te", bv)])
    RC = 12582912.0
    for rg in range(RG):
        for cc in range(NCC):
            c0, c1 = cc * CW, min((cc + 1) * CW, C)
            cw = c1 - c0
            tk = p.dma("sync", st0[:, 0:cw], w_ap[rg * 128:(rg + 1) * 128, c0:c1],
                       waits=["ve", "se", "te"])
            p.op("vector", lambda e, cw=cw: e.tensor_scalar_mul(st0[:, 0:cw], st0[:, 0:cw], gam[:, 0:1]), waits=[tk])
            p.op("vector", lambda e, cw=cw: e.tensor_scalar_add(st0[:, 0:cw], st0[:, 0:cw], RC))
            p.op("vector", lambda e, cw=cw: e.tensor_scalar_sub(st0[:, 0:cw], st0[:, 0:cw], RC))
            p.op("vector", lambda e, cw=cw: e.tensor_scalar_min(st0[:, 0:cw], st0[:, 0:cw], 1.0))
            p.op("vector", lambda e, cw=cw: e.tensor_scalar_max(st0[:, 0:cw], st0[:, 0:cw], -1.0))
            p.op("vector", lambda e, cw=cw: e.tensor_copy(st1.bitcast(BF16)[:, 0:cw], st0[:, 0:cw]))
            qv = p.mark("ve")
            for cg in range(cw // 128):
                tm = p.op("tensor", lambda e, cg=cg: e.transpose(
                    wpt[:, :], st1.bitcast(BF16)[:, cg * 128:(cg + 1) * 128], ident[:]),
                    waits=[("ve", qv), "ve"], inc="te")
                p.op("vector", lambda e, rg=rg, cg=cg, c0=c0: e.tensor_copy(
                    wT[:, c0 // 128 + cg, rg * 128:(rg + 1) * 128], wpt[:, :]),
                    waits=[("te", tm)])
    return wT


def build_l1_ffn():
    nc = bass.Bass("TRN2", target_bir_lowering=False)
    x_in = nc.dram_tensor("x_in", [D, T], F32, kind="ExternalInput").ap()
    w1 = nc.dram_tensor("w1", [4 * D, D], F32, kind="ExternalInput").ap()
    b1 = nc.dram_tensor("b1", [4 * D, 1], F32, kind="ExternalInput").ap()
    w2 = nc.dram_tensor("w2", [D, 4 * D], F32, kind="ExternalInput").ap()
    b2 = nc.dram_tensor("b2", [D, 1], F32, kind="ExternalInput").ap()
    x_out = nc.dram_tensor("x_out", [D, T], F32, kind="ExternalOutput").ap()
    F = 4 * D
    FG = F // 128

    with contextlib.ExitStack() as ctx:
        p = P(nc, ctx)
        p.ones_row = p.sbuf("p_ones_row", [1, 128], F32)
        p.op("vector", lambda e: e.memset(p.ones_row[:], 1.0))
        ident = load_ident(p)
        stage = p.sbuf("stage", [128, 2 * 2048], F32)
        wpt = p.psum("wpt", [128, 128], BF16)
        wsh = {"acc": p.sbuf("q_acc", [128, 1], F32), "gam": p.sbuf("q_gam", [128, 1], F32),
               "pr": p.psum("q_pr", [1, 1], F32), "pc": p.psum("q_pc", [128, 1], F32),
               "on": p.sbuf("q_on", [128, 1], F32)}
        p.op("vector", lambda e: e.memset(wsh["on"][:], 1.0))
        w1T = wload_quant_T(p, "w1", w1, F, D, ident, stage, wpt, shared=wsh)
        w2T = wload_quant_T(p, "w2", w2, D, F, ident, stage, wpt, shared=wsh)

        sb1 = p.sbuf("sb1", [128, FG], F32)
        sb2 = p.sbuf("sb2", [128, G], F32)
        t1 = p.dma("sync", sb1[:, :], b1.rearrange("(g p) c -> p (g c)", p=128),
                   allow_slow_non_contiguous=True)
        t2 = p.dma("sync", sb2[:, :], b2.rearrange("(g p) c -> p (g c)", p=128),
                   allow_slow_non_contiguous=True)
        p.wait_only("scalar", [t1, t2])

        TBL = 256; NTBL = T // TBL
        xb = p.sbuf("xb", [128, G, TBL], F32)
        xbh = p.sbuf("xbh", [128, G, TBL], BF16)
        hh = p.sbuf("hh", [128, FG, TBL], BF16)
        yy = p.sbuf("yy", [128, G, TBL], F32)
        pmm = [p.psum(f"pmm{i}", [128, TBL], F32) for i in range(2)]
        cm = make_common(p, TBL)

        store_tok = None
        for ib in range(NTBL):
            t0 = ib * TBL
            if store_tok is not None:
                p.wait_only("vector", [store_tok])
            xld = p.dma("sync", xb[:], fm_src(x_in, slice(t0, t0 + TBL)),
                        waits=["ve", "se", "te"])
            p.wait_only("vector", [xld])
            for g in range(G):
                p.op("vector", lambda e, g=g: e.tensor_copy(xbh[:, g, :], xb[:, g, :]),
                     waits=[xld])
            xv = p.mark("ve")
            xc = p.sbuf(f"xc{ib}", [1, 1], F32) if False else None
            def ev_h(og, bank, mm):
                p.op("scalar", lambda e, og=og, bank=bank: e.activation(
                    hh[:, og, :], bank[:, 0:TBL], AF.Relu, bias=sb1[:, og:og + 1]),
                    waits=[("te", mm)])
            proj_fm(p, w1T, xbh, TBL, range(FG), G, pmm, ev_h, first_waits=[("ve", xv)])
            def ev_y(og, bank, mm):
                p.op("scalar", lambda e, og=og, bank=bank: e.activation(
                    cm["sq"][:, og * TBL:(og + 1) * TBL], bank[:, 0:TBL], AF.Identity,
                    bias=sb2[:, og:og + 1]), waits=[("te", mm)])
                p.op("vector", lambda e, og=og: e.tensor_add(
                    yy[:, og, :], cm["sq"][:, og * TBL:(og + 1) * TBL], xb[:, og, :]),
                    waits=["se"])
            proj_fm(p, w2T, hh, TBL, range(G), FG, pmm, ev_y)
            emit_ln_fm(p, yy, yy, cm, TBL)
            store_tok = p.dma("sync", fm_src(x_out, slice(t0, t0 + TBL)), yy[:],
                              waits=["ve"])
        for qn in list(p.dqueue):
            p.dma_flush(qn)
        p.finish()
    return nc


def build_l2_attn():
    nc = bass.Bass("TRN2", target_bir_lowering=False)
    x_in = nc.dram_tensor("x_in", [D, T], F32, kind="ExternalInput").ap()
    wq = nc.dram_tensor("wq", [D, D], F32, kind="ExternalInput").ap()
    bq = nc.dram_tensor("bq", [D, 1], F32, kind="ExternalInput").ap()
    wk = nc.dram_tensor("wk", [D, D], F32, kind="ExternalInput").ap()
    bk = nc.dram_tensor("bk", [D, 1], F32, kind="ExternalInput").ap()
    wv = nc.dram_tensor("wv", [D, D], F32, kind="ExternalInput").ap()
    bv = nc.dram_tensor("bv", [1, D], F32, kind="ExternalInput").ap()
    wp = nc.dram_tensor("wp", [D, D], F32, kind="ExternalInput").ap()
    bp = nc.dram_tensor("bp", [D, 1], F32, kind="ExternalInput").ap()
    x_out = nc.dram_tensor("x_out", [D, T], F32, kind="ExternalOutput").ap()

    NH = 16
    KC = 3 * T // 4
    NKT = KC // 128
    CT = 512
    CC = 384
    SCL = 1.0 / 8.0

    with contextlib.ExitStack() as ctx:
        p = P(nc, ctx)
        p.ones_row = p.sbuf("p_ones_row", [1, 128], F32)
        p.op("vector", lambda e: e.memset(p.ones_row[:], 1.0))
        onescol = p.sbuf("onescol", [128, 1], F32)
        p.op("vector", lambda e: e.memset(onescol[:], 1.0))

        wbA = p.sbuf("wbA", [128, G, D], BF16)
        wbB = p.sbuf("wbB", [128, G, D], BF16)
        wks, wvs = wbA, wbB
        tka = p.dma("gpsimd", wbA[:], wk.rearrange("(g p) c -> p g c", p=128))
        tkb = p.dma("gpsimd", wbB[:], wv.rearrange("(g p) c -> p g c", p=128))
        p.wait_only("tensor", [tka, tkb])
        sbq = p.sbuf("sbq", [128, G], F32)
        sbk = p.sbuf("sbk", [128, G], F32)
        sbp = p.sbuf("sbp", [128, G], F32)
        bvrow = p.sbuf("bvrow", [1, D], F32)
        tks = [p.dma("sync", sbq[:, :], bq.rearrange("(g p) c -> p (g c)", p=128),
                     allow_slow_non_contiguous=True),
               p.dma("sync", sbk[:, :], bk.rearrange("(g p) c -> p (g c)", p=128),
                     allow_slow_non_contiguous=True),
               p.dma("sync", sbp[:, :], bp.rearrange("(g p) c -> p (g c)", p=128),
                     allow_slow_non_contiguous=True),
               p.dma("sync", bvrow[:], bv)]
        wld = [p.dqueue["gpsimd"]]
        p.wait_only("scalar", tks)

        Kc = p.sbuf("Kc", [128, G, KC], BF16)
        Vc = p.sbuf("Vc", [128, NKT, NH, 65], BF16)
        p.op("vector", lambda e: e.memset(Vc[:, :, :, 64:65], 1.0))
        maxk2 = p.sbuf("maxk2", [1, NH], F32)
        p.op("vector", lambda e: e.memset(maxk2[:], 0.0))
        hk2 = p.sbuf("hk2", [1, CC], F32)
        hk1 = p.sbuf("hk1", [1, 1], F32)

        xb = p.sbuf("xb", [128, G, CT], F32)
        xcc = p.sbuf("xcc", [128, G, CC], BF16)
        sqs = p.sbuf("sqs", [128, CT], F32)
        ps = [p.psum(f"ps{i}", [128, 512], F32) for i in range(2)]
        pav = p.psum("pav", [65, 512], F32)
        pxtra = p.psum("pxtra", [128, 512], F32)
        pb1 = pxtra[0:1, 0:512]
        pbc = pxtra[:, 0:1]

        for ci in range(T // CT):
            t0 = ci * CT
            xld = p.dma("sync", xb[:], fm_src(x_in, slice(t0, t0 + CT)),
                        waits=["ve", "se", "te"])
            for g in range(G):
                p.op("vector", lambda e, g=g: e.tensor_copy(
                    xcc[:, g, :].rearrange("p (a b) -> p a b", b=3),
                    xb[:, g, :].rearrange("p (a b) -> p a b", b=4)[:, :, 0:3]),
                    waits=[xld])
            xcv = p.mark("ve")
            for og in range(G):
                bank = ps[og % 2]
                mm = None
                for kg in range(G):
                    mm = p.op("tensor", lambda e, og=og, kg=kg, bank=bank: e.matmul(
                        bank[:, 0:CC], wks[:, kg, og * 128:(og + 1) * 128],
                        xcc[:, kg, :], start=(kg == 0), stop=(kg == G - 1)),
                        waits=[("ve", xcv), "se", "ve"] if kg == 0 else [], inc="te")
                p.op("scalar", lambda e, og=og, bank=bank, ci=ci: e.activation(
                    Kc[:, og, ci * CC:(ci + 1) * CC], bank[:, 0:CC], AF.Identity,
                    bias=sbk[:, og:og + 1]), waits=[("te", mm)])
                p.op("scalar", lambda e, og=og, bank=bank: e.activation(
                    sqs[:, 0:CC], bank[:, 0:CC], AF.Square), waits=["se"])
                for hh_ in range(2):
                    h = og * 2 + hh_
                    sv = p.mark("se")
                    mm2 = p.op("tensor", lambda e, hh_=hh_: e.matmul(
                        pb1[0:1, 0:CC], onescol[64 * hh_:64 * (hh_ + 1), 0:1],
                        sqs[64 * hh_:64 * (hh_ + 1), 0:CC], start=True, stop=True),
                        waits=[("se", sv), "ve"], inc="te")
                    p.op("vector", lambda e: e.reduce_max(hk1[0:1, 0:1], pb1[0:1, 0:CC],
                                                          axis=AX.X), waits=[("te", mm2)])
                    p.op("vector", lambda e, h=h: e.tensor_max(
                        maxk2[0:1, h:h + 1], maxk2[0:1, h:h + 1], hk1[0:1, 0:1]))
            for ti in range(3):
                kt = ci * 3 + ti
                for hg in range(2):
                    bank = ps[hg]
                    mm = None
                    for kg in range(G):
                        mm = p.op("tensor", lambda e, ti=ti, kg=kg, hg=hg, bank=bank: e.matmul(
                            bank[:, 0:512], xcc[:, kg, ti * 128:(ti + 1) * 128],
                            wvs[:, kg, hg * 512:(hg + 1) * 512],
                            start=(kg == 0), stop=False),
                            waits=["se", "ve"] if kg == 0 else [], inc="te")
                    mm = p.op("tensor", lambda e, hg=hg, bank=bank: e.matmul(
                        bank[:, 0:512], p.ones_row[0:1, 0:128],
                        bvrow[0:1, hg * 512:(hg + 1) * 512], start=False, stop=True),
                        waits=[], inc="te")
                    p.op("vector", lambda e, kt=kt, hg=hg, bank=bank: e.tensor_copy(
                        Vc[:, kt, hg * 8:(hg + 1) * 8, 0:64],
                        bank[:, 0:512].rearrange("p (h c) -> p h c", c=64)),
                        waits=[("te", mm)])

        wqs, wps = wbA, wbB
        tka = p.dma("gpsimd", wbA[:], wq.rearrange("(g p) c -> p g c", p=128),
                    waits=["te", "se", "ve"])
        tkb = p.dma("gpsimd", wbB[:], wp.rearrange("(g p) c -> p g c", p=128),
                    waits=["te", "se", "ve"])
        p.wait_only("tensor", [tka, tkb])
        Q2 = p.sbuf("Q2", [128, G, TB], BF16)
        xbh2 = p.sbuf("xbh2", [128, G, TB], BF16)
        sq2 = sqs
        bnegc = p.sbuf("bnegc", [128, 1], F32)
        brow = p.sbuf("brow", [1, TB], F32)
        bsc = p.sbuf("bsc", [1, 1], F32)
        pt = [p.sbuf(f"pt{i}", [128, TB], BF16) for i in range(2)]
        att = p.sbuf("att", [128, G, TB], BF16)
        qiota = p.sbuf("qiota", [128, TB], F32)
        origc = [p.sbuf(f"origc{i}", [128, 1], F32) for i in range(3)]
        maskt = [p.sbuf(f"maskt{i}", [128, TB], F32) for i in range(4)]
        rden = p.sbuf("rden", [1, TB], F32)
        pdiv = pxtra[0:64, 0:TB]
        p.shared_sq = sqs
        cm = make_common(p, TB, slim=True)
        p.op("gpsimd", lambda e: e.iota(qiota[:], [[1, TB]], base=0,
                                        channel_multiplier=0,
                                        allow_small_or_imprecise_dtypes=True))
        rel = np.zeros((3, 128), np.float32)
        for ph in range(3):
            for r in range(128):
                rel[ph, r] = 4 * ((ph + r) // 3) + (ph + r) % 3
        origc_toks = []
        for ph in range(3):
            c_ = nc.inline_tensor(rel[ph].reshape(128, 1), f"rel{ph}").ap()
            origc_toks.append(p.dma("sync", origc[ph][:], c_))

        store_tok = None
        for qb in range(NTB):
            q0 = qb * TB
            xld = p.dma("sync", xb[:, :, 0:TB], fm_src(x_in, slice(q0, q0 + TB)),
                        waits=(["ve", "se", "te"] + ([store_tok] if store_tok else [])))
            p.wait_only("vector", [xld])
            for g in range(G):
                p.op("vector", lambda e, g=g: e.tensor_copy(xbh2[:, g, :], xb[:, g, 0:TB]))
            for og in range(G):
                bank = ps[og % 2]
                mm = None
                for kg in range(G):
                    mm = p.op("tensor", lambda e, og=og, kg=kg, bank=bank: e.matmul(
                        bank[:, 0:TB], wqs[:, kg, og * 128:(og + 1) * 128],
                        xbh2[:, kg, :], start=(kg == 0), stop=(kg == G - 1)),
                        waits=[xld, "se", "ve"] if kg == 0 else [], inc="te")
                p.op("scalar", lambda e, og=og, bank=bank: e.activation(
                    Q2[:, og, :], bank[:, 0:TB], AF.Identity,
                    bias=sbq[:, og:og + 1]), waits=[("te", mm)])
            kt_lo = (3 * q0) // 4 // 128
            kt_hi = min(NKT, (3 * (q0 + TB) + 511) // 4 // 128 + 1)
            nmask = 0
            mtiles = {}
            for kt in range(kt_lo, kt_hi):
                a = 128 * kt
                phi = a % 3
                origbase = 4 * ((a - phi) // 3)
                delta = float(q0 - origbase)
                mt = maskt[nmask % 4]
                nmask += 1
                mtiles[kt] = mt
                p.op("vector", lambda e, mt=mt, phi=phi, delta=delta: e.tensor_scalar(
                    mt[:], qiota[:, 0:TB], origc[phi][:, 0:1], float(delta),
                    ALU.subtract, ALU.add),
                    waits=["se", "te", "ge"] + origc_toks)
                p.op("vector", lambda e, mt=mt: e.tensor_single_scalar(
                    mt[:], mt[:], 0.0, ALU.is_lt))
                p.op("vector", lambda e, mt=mt: e.tensor_scalar_mul(
                    mt[:], mt[:], -3.0e38))
            mskv = p.mark("ve")
            for h in range(NH):
                og, hh_ = h // 2, h % 2
                r0_, r1_ = 64 * hh_, 64 * (hh_ + 1)
                sv0 = p.op("scalar", lambda e, og=og: e.activation(
                    sq2[:, :], Q2[:, og, :], AF.Square), waits=["ve"])
                mmb = p.op("tensor", lambda e, r0_=r0_, r1_=r1_: e.matmul(
                    pb1[0:1, 0:TB], onescol[r0_:r1_, 0:1], sq2[r0_:r1_, 0:TB],
                    start=True, stop=True), waits=[("se", sv0), "ve"], inc="te")
                p.op("vector", lambda e: e.reduce_max(bsc[0:1, 0:1], pb1[0:1, 0:TB],
                                                      axis=AX.X), waits=[("te", mmb)])
                p.op("vector", lambda e, h=h: e.tensor_tensor(
                    bsc[0:1, 0:1], bsc[0:1, 0:1], maxk2[0:1, h:h + 1], ALU.mult))
                p.op("scalar", lambda e: e.activation(bsc[0:1, 0:1], bsc[0:1, 0:1],
                                                      AF.Sqrt), waits=["ve"])
                p.op("vector", lambda e: e.tensor_scalar_mul(bsc[0:1, 0:1],
                                                             bsc[0:1, 0:1], -SCL),
                     waits=["se"])
                bmm = p.op("tensor", lambda e: e.matmul(
                    pbc[:, 0:1], p.ones_row[0:1, :], bsc[0:1, 0:1],
                    start=True, stop=True), waits=["ve"], inc="te")
                p.op("vector", lambda e: e.tensor_copy(bnegc[:], pbc[:, 0:1]),
                     waits=[("te", bmm)])
                bnv = p.mark("ve")
                kt_end = min(NKT, (3 * (q0 + TB) + 3) // 4 // 128 + 1)
                avm = None
                for kt in range(kt_end):
                    bank = ps[kt % 2]
                    smm = p.op("tensor", lambda e, kt=kt, og=og, r0_=r0_, r1_=r1_, bank=bank: e.matmul(
                        bank[:, 0:TB], Kc[r0_:r1_, og, kt * 128:(kt + 1) * 128],
                        Q2[r0_:r1_, og, :], start=True, stop=True),
                        waits=[("ve", bnv), ("se", p.tot["se"])], inc="te")
                    if kt in mtiles:
                        p.op("vector", lambda e, mt=mtiles[kt], bank=bank: e.tensor_tensor(
                            bank[:, 0:TB], bank[:, 0:TB], mt[:], ALU.add),
                            waits=[("te", smm), ("ve", mskv)])
                        p.wait_only("scalar", ["ve"])
                    ex = p.op("scalar", lambda e, kt=kt, bank=bank: e.activation(
                        pt[kt % 2][:], bank[:, 0:TB], AF.Exp,
                        bias=bnegc[:, 0:1], scale=SCL), waits=[("te", smm)])
                    avm = p.op("tensor", lambda e, kt=kt, h=h, kt_end=kt_end: e.matmul(
                        pav[:, 0:TB], Vc[:, kt, h, :], pt[kt % 2][:],
                        start=(kt == 0), stop=(kt == kt_end - 1)),
                        waits=([("se", ex)] + (["ve"] if kt == 0 else [])), inc="te")
                p.op("vector", lambda e: e.reciprocal(rden[0:1, :], pav[64:65, 0:TB]),
                     waits=[("te", avm)])
                rv = p.mark("ve")
                dmm = p.op("tensor", lambda e: e.matmul(
                    pdiv[:, 0:TB], p.ones_row[0:1, 0:64], rden[0:1, 0:TB],
                    start=True, stop=True), waits=[("ve", rv)], inc="te")
                p.op("vector", lambda e, og=og, r0_=r0_: e.tensor_copy(
                    sq2[r0_:r0_ + 64, 0:TB], pav[0:64, 0:TB]), waits=[("te", dmm)])
                p.op("vector", lambda e, og=og, r0_=r0_: e.tensor_tensor(
                    att[r0_:r0_ + 64, og, :], sq2[r0_:r0_ + 64, 0:TB], pdiv[0:64, 0:TB],
                    ALU.mult))
            atv = p.mark("ve")
            for og in range(G):
                bank = ps[og % 2]
                mm = None
                for kg in range(G):
                    mm = p.op("tensor", lambda e, og=og, kg=kg, bank=bank: e.matmul(
                        bank[:, 0:TB], wps[:, kg, og * 128:(og + 1) * 128],
                        att[:, kg, :], start=(kg == 0), stop=(kg == G - 1)),
                        waits=[("ve", atv), "se"] if kg == 0 else [], inc="te")
                p.op("scalar", lambda e, og=og, bank=bank: e.activation(
                    sqs[:, 0:TB], bank[:, 0:TB], AF.Identity, bias=sbp[:, og:og + 1]),
                    waits=[("te", mm), "ve"])
                p.op("vector", lambda e, og=og: e.tensor_add(
                    xb[:, og, 0:TB], sqs[:, 0:TB], xb[:, og, 0:TB]), waits=["se"])
            emit_ln_fm(p, xb[:, :, 0:TB] if TB == CT else xb, xb, cm, TB)
            store_tok = p.dma("sync", fm_src(x_out, slice(q0, q0 + TB)), xb[:, :, 0:TB],
                              waits=["ve"])
        for qn in list(p.dqueue):
            p.dma_flush(qn)
        p.finish()
    return nc


def build_l0_attn():
    nc = bass.Bass("TRN2", target_bir_lowering=False)
    x_in = nc.dram_tensor("x_in", [D, T], F32, kind="ExternalInput").ap()
    wq = nc.dram_tensor("wq", [D, D], F32, kind="ExternalInput").ap()
    wk = nc.dram_tensor("wk", [D, D], F32, kind="ExternalInput").ap()
    wv = nc.dram_tensor("wv", [D, D], F32, kind="ExternalInput").ap()
    wo = nc.dram_tensor("wo", [D, D], F32, kind="ExternalInput").ap()
    x_out = nc.dram_tensor("x_out", [D, T], F32, kind="ExternalOutput").ap()
    khbm = nc.dram_tensor("khbm", [128, G, T], BF16, kind="Internal").ap()
    vhbm = nc.dram_tensor("vhbm", [T // 128, 128, D], BF16, kind="Internal").ap()
    mhbm = nc.dram_tensor("mhbm", [64, 128], F32, kind="Internal").ap()

    NH = 16
    SCL = 1.0 / 8.0
    NKT = T // 128

    with contextlib.ExitStack() as ctx:
        p = P(nc, ctx)
        p.ones_row = p.sbuf("p_ones_row", [1, 128], F32)
        p.op("vector", lambda e: e.memset(p.ones_row[:], 1.0))
        onescol = p.sbuf("onescol", [128, 1], F32)
        p.op("vector", lambda e: e.memset(onescol[:], 1.0))
        onescolh = p.sbuf("onescolh", [128, 1], BF16)
        p.op("vector", lambda e: e.memset(onescolh[:], 1.0))

        wbA = p.sbuf("wbA", [128, G, D], BF16)
        wbB = p.sbuf("wbB", [128, G, D], BF16)
        tka = p.dma("gpsimd", wbA[:], wk.rearrange("(g p) c -> p g c", p=128))
        tkb = p.dma("gpsimd", wbB[:], wv.rearrange("(g p) c -> p g c", p=128))
        p.wait_only("tensor", [tka, tkb])

        xb = p.sbuf("xb", [128, G, TB], F32)
        xbh = p.sbuf("xbh", [128, G, TB], BF16)
        kstage = p.sbuf("kstage", [128, G, TB], BF16)
        vstage = p.sbuf("vstage", [128, 4, D], BF16)
        vtot = p.sbuf("vtot", [128, NH, 1], F32)
        vtrow = p.sbuf("vtrow", [1, D], F32)
        ps = [p.psum(f"ps{i}", [128, 512], F32) for i in range(2)]
        pbig = p.psum("pbig", [128, 2 * TB], F32)
        pvt = pbig[0:1, 0:D]

        st_k = None
        st_v = None
        for ci in range(NTB):
            t0 = ci * TB
            w_ = [st_k] if st_k else []
            xld = p.dma("sync", xb[:], fm_src(x_in, slice(t0, t0 + TB)),
                        waits=["ve", "se", "te"])
            p.wait_only("vector", [xld])
            for g in range(G):
                p.op("vector", lambda e, g=g: e.tensor_copy(xbh[:, g, :], xb[:, g, :]))
            xv = p.mark("ve")
            for og in range(G):
                bank = ps[og % 2]
                mm = None
                for kg in range(G):
                    mm = p.op("tensor", lambda e, og=og, kg=kg, bank=bank: e.matmul(
                        bank[:, 0:TB], wbA[:, kg, og * 128:(og + 1) * 128],
                        xbh[:, kg, :], start=(kg == 0), stop=(kg == G - 1)),
                        waits=[("ve", xv), "se", "ve"] if kg == 0 else [], inc="te")
                p.op("scalar", lambda e, og=og, bank=bank: e.activation(
                    kstage[:, og, :], bank[:, 0:TB], AF.Identity), waits=[("te", mm), "ve"])
            kse = p.mark("se")
            st_k = p.dma("sync", khbm[:, :, t0:t0 + TB], kstage[:],
                         waits=[("se", kse)] + w_)
            for ti in range(4):
                for hg in range(2):
                    bank = ps[hg]
                    mm = None
                    for kg in range(G):
                        mm = p.op("tensor", lambda e, ti=ti, kg=kg, hg=hg, bank=bank: e.matmul(
                            bank[:, 0:512], xbh[:, kg, ti * 128:(ti + 1) * 128],
                            wbB[:, kg, hg * 512:(hg + 1) * 512],
                            start=(kg == 0), stop=(kg == G - 1)),
                            waits=["se", "ve"] if kg == 0 else [], inc="te")
                    p.op("vector", lambda e, ti=ti, hg=hg, bank=bank: e.tensor_copy(
                        vstage[:, ti, hg * 512:(hg + 1) * 512], bank[:, 0:512]),
                        waits=[("te", mm)])
                    p.op("tensor", lambda e, ti=ti, hg=hg, bank=bank, ci=ci: e.matmul(
                        pvt[0:1, hg * 512:(hg + 1) * 512], onescolh[:, 0:1],
                        vstage[:, ti, hg * 512:(hg + 1) * 512],
                        start=(ci == 0 and ti == 0), stop=(ci == NTB - 1 and ti == 3)),
                        waits=["ve"], inc="te")
            vse = p.mark("ve")
            w_ = [st_v] if st_v else []
            st_v = p.dma("sync", vhbm[ci * 4:(ci + 1) * 4, :, :].rearrange("k p c -> p k c"),
                         vstage[:], waits=[("ve", vse)] + w_)
        tvt = p.mark("te")
        p.op("vector", lambda e: e.tensor_copy(vtrow[:], pvt[0:1, :]), waits=[("te", tvt)])
        vr = p.mark("ve")
        pvc = ps[0]
        for h in range(NH):
            mm = p.op("tensor", lambda e, h=h: e.matmul(
                pvc[0:64, h:h + 1], vtrow[0:1, h * 64:(h + 1) * 64],
                p.ones_row[0:1, 0:1], start=True, stop=True),
                waits=[("ve", vr), "ve"], inc="te")
            p.op("vector", lambda e, h=h, mm=mm: e.tensor_copy(
                vtot[0:64, h, 0:1], pvc[0:64, h:h + 1]), waits=[("te", mm)])

        wqs, wos = wbA, wbB
        tka = p.dma("gpsimd", wbA[:], wq.rearrange("(g p) c -> p g c", p=128),
                    waits=["te", "se", "ve"])
        tkb = p.dma("gpsimd", wbB[:], wo.rearrange("(g p) c -> p g c", p=128),
                    waits=["te", "se", "ve"])
        p.wait_only("tensor", [tka, tkb])

        Q2 = p.sbuf("Q2", [128, G, TB], BF16)
        kch = [p.sbuf(f"kch{i}", [128, G, TB], BF16) for i in range(2)]
        kwin = p.sbuf("kwin", [128, G, 768], BF16)
        vwin = p.sbuf("vwin", [128, 6, D], BF16)
        mall = p.sbuf("mall", [128, 64], F32)
        mtmp = p.sbuf("mtmp", [128, 1], F32)
        mstag = p.sbuf("mstag", [1, 64 * 128], F32)
        mtr = p.sbuf("mtr", [64, 128], F32)
        eE = p.sbuf("eE", [128, TB], F32)
        p2b = [p.sbuf(f"p2b{i}", [128, TB], BF16) for i in range(2)]
        att = p.sbuf("att", [128, G, TB], BF16)
        denr = p.sbuf("denr", [1, TB], F32)
        denb = p.sbuf("denb", [64, TB], F32)
        mbs = p.sbuf("mbs", [128, TB], F32)
        sqs = p.sbuf("sqs", [128, TB], F32)
        pav = p.psum("pav", [64, 512], F32)
        pden = p.psum("pden", [1, 512], F32)
        pmb = pbig[:, 0:TB]
        p.shared_sq = sqs
        p.shared_prow = pbig[0:1, 0:2 * TB]
        p.shared_pbc = pbig
        cm = make_common(p, TB, slim=True)
        wmask = []
        iq = np.arange(TB)
        for wi, dt_ in enumerate((-128, 0, 128, 256, 384, 512)):
            r = np.arange(128)
            m = ((dt_ + r[:, None] - iq[None, :]) <= 127) & ((dt_ + r[:, None] - iq[None, :]) >= -128)
            c_ = nc.inline_tensor(m.astype(np.float32), f"wmask{wi}").ap()
            mb = p.sbuf(f"wmaskb{wi}", [128, TB], F32)
            wmask.append((mb, p.dma("sync", mb[:], c_)))

        store_tok = None
        for qb in range(NTB):
            q0 = qb * TB
            xld = p.dma("sync", xb[:, :, 0:TB], fm_src(x_in, slice(q0, q0 + TB)),
                        waits=(["ve", "se", "te"] + ([store_tok] if store_tok else [])))
            p.wait_only("vector", [xld])
            for g in range(G):
                p.op("vector", lambda e, g=g: e.tensor_copy(xbh[:, g, :], xb[:, g, 0:TB]))
            xv = p.mark("ve")
            for og in range(G):
                bank = ps[og % 2]
                mm = None
                for kg in range(G):
                    mm = p.op("tensor", lambda e, og=og, kg=kg, bank=bank: e.matmul(
                        bank[:, 0:TB], wqs[:, kg, og * 128:(og + 1) * 128],
                        xbh[:, kg, :], start=(kg == 0), stop=(kg == G - 1)),
                        waits=[("ve", xv), "se", "ve"] if kg == 0 else [], inc="te")
                p.op("scalar", lambda e, og=og, bank=bank: e.activation(
                    Q2[:, og, :], bank[:, 0:TB], AF.Identity), waits=[("te", mm), "ve"])
            qv = p.mark("se")
            p.op("vector", lambda e: e.memset(mall[:], -3.0e38), waits=["te"])
            kld = None
            for cc in range(NTB):
                kbuf = kch[cc % 2]
                w_ = ["ve"] if cc >= 2 else []
                kld = p.dma("sync", kbuf[:], khbm[:, :, cc * TB:(cc + 1) * TB],
                            waits=w_)
                for h in range(NH):
                    og, hh_ = h // 2, h % 2
                    r0_ = 64 * hh_
                    for sub in range(4):
                        bank = ps[(h * 4 + sub) % 2]
                        mm = p.op("tensor", lambda e, og=og, r0_=r0_, sub=sub, bank=bank, kbuf=kbuf: e.matmul(
                            bank[:, 0:TB], Q2[r0_:r0_ + 64, og, sub * 128:(sub + 1) * 128],
                            kbuf[r0_:r0_ + 64, og, :], start=True, stop=True),
                            waits=[kld, ("se", qv), "ve"], inc="te")
                        p.op("vector", lambda e, bank=bank: e.reduce_max(
                            mtmp[:], bank[:, 0:TB], axis=AX.X), waits=[("te", mm)])
                        p.op("vector", lambda e, h=h, sub=sub: e.tensor_max(
                            mall[:, h * 4 + sub:h * 4 + sub + 1],
                            mall[:, h * 4 + sub:h * 4 + sub + 1], mtmp[:]))
            mv = p.mark("ve")
            idh = p.sbuf("idh", [128, 128], BF16) if qb == 0 else idh
            if qb == 0:
                idc = nc.inline_tensor(np.eye(128, dtype=ml_dtypes.bfloat16), "identL0").ap()
                idtok = p.dma("sync", idh[:], idc)
                p.wait_only("tensor", [idtok])
            mmt = p.op("tensor", lambda e: e.matmul(
                pmb[:, 0:128], mall[:, 0:64].bitcast(F32), idh[0:64, :].bitcast(BF16),
                start=True, stop=True), waits=[("ve", mv)], inc="te") if False else None
            if qb == 0:
                idf = p.sbuf("idf", [128, 128], F32)
                idctok = p.dma("sync", idf[:], nc.inline_tensor(np.eye(128, dtype=np.float32), "identF").ap())
                p.wait_only("tensor", [idctok])
            tmt = p.op("tensor", lambda e: e.transpose(
                pmb[0:64, 0:128], mall[:, 0:64], idf[:]), waits=[("ve", mv)], inc="te")
            p.op("vector", lambda e: e.tensor_copy(mtr[:], pmb[0:64, 0:128]),
                 waits=[("te", tmt)])
            mcv = p.mark("ve")
            mdma1 = p.dma("gpsimd", mhbm, mtr[:], waits=[("ve", mcv)])
            mdma = p.dma("gpsimd", mstag[:], mhbm.rearrange("(x a) c -> x (a c)", x=1),
                         waits=[mdma1])
            p.op("vector", lambda e: e.tensor_scalar_mul(mstag[:], mstag[:], -1.0),
                 waits=[mdma])
            negv = p.mark("ve")
            for h in range(NH):
                og, hh_ = h // 2, h % 2
                r0_ = 64 * hh_
                if h == 0:
                    kt_lo = max(0, qb * 4 - 1)
                    kt_hi = min(NKT, qb * 4 + 5)
                    w0 = kt_lo * 128
                    kwl = p.dma("sync", kwin[:, :, 0:(kt_hi - kt_lo) * 128],
                                khbm[:, :, w0:kt_hi * 128], waits=["te", "se"])
                    vwl = p.dma("sync", vwin[:, 0:kt_hi - kt_lo, :],
                                vhbm[kt_lo:kt_hi, :, :].rearrange("k p c -> p k c"),
                                waits=["te", "se"])
                    p.wait_only("tensor", [kwl, vwl])
                    p.wait_only("scalar", [kwl, vwl])
                bmm = p.op("tensor", lambda e, h=h: e.matmul(
                    pmb[:, 0:TB], p.ones_row[0:1, :],
                    mstag[0:1, h * 512:(h + 1) * 512], start=True, stop=True),
                    waits=[("ve", negv), "ve"], inc="te")
                p.op("vector", lambda e: e.tensor_copy(mbs[:], pmb[:, 0:TB]),
                     waits=[("te", bmm)])
                nwin = kt_hi - kt_lo
                avm = None
                dnm = None
                for wi in range(nwin):
                    kt = kt_lo + wi
                    mi = kt * 128 - q0
                    mask_idx = (mi + 128) // 128
                    bank = ps[wi % 2]
                    smm = p.op("tensor", lambda e, wi=wi, og=og, r0_=r0_, bank=bank: e.matmul(
                        bank[:, 0:TB], kwin[r0_:r0_ + 64, og, wi * 128:(wi + 1) * 128],
                        Q2[r0_:r0_ + 64, og, :], start=True, stop=True),
                        waits=[("te", bmm), "ve", "se"], inc="te")
                    p.op("vector", lambda e, bank=bank: e.scalar_tensor_tensor(
                        eE[:], bank[:, 0:TB], SCL, mbs[:], ALU.mult, ALU.add),
                        waits=[("te", smm)])
                    p.op("scalar", lambda e: e.activation(eE[:], eE[:], AF.Exp),
                         waits=["ve"])
                    p.op("scalar", lambda e: e.activation(eE[:], eE[:], AF.Exp),
                         waits=["se"])
                    p.op("vector", lambda e, wi=wi, mask_idx=mask_idx: e.scalar_tensor_tensor(
                        p2b[wi % 2][:], eE[:], -1.0, wmask[mask_idx][0][:],
                        ALU.add, ALU.mult), waits=["se", wmask[mask_idx][1]])
                    pv_ = p.mark("ve")
                    dnm = p.op("tensor", lambda e, wi=wi, nwin=nwin: e.matmul(
                        pden[0:1, 0:TB], onescolh[:, 0:1], p2b[wi % 2][:],
                        start=(wi == 0), stop=(wi == nwin - 1)),
                        waits=[("ve", pv_)], inc="te")
                    avm = p.op("tensor", lambda e, wi=wi, h=h, nwin=nwin: e.matmul(
                        pav[:, 0:TB], vwin[:, wi, h * 64:(h + 1) * 64], p2b[wi % 2][:],
                        start=(wi == 0), stop=(wi == nwin - 1)),
                        waits=[], inc="te")
                p.op("vector", lambda e: e.tensor_scalar_add(denr[0:1, :], pden[0:1, 0:TB],
                                                             float(T)), waits=[("te", avm), ("te", dnm)])
                p.op("vector", lambda e: e.reciprocal(denr[0:1, :], denr[0:1, :]))
                rv = p.mark("ve")
                dmm = p.op("tensor", lambda e: e.matmul(
                    pmb[0:64, 0:TB], p.ones_row[0:1, 0:64], denr[0:1, 0:TB],
                    start=True, stop=True), waits=[("ve", rv)], inc="te")
                p.op("vector", lambda e: e.tensor_copy(denb[:], pmb[0:64, 0:TB]),
                     waits=[("te", dmm)])
                p.op("vector", lambda e, h=h, og=og, r0_=r0_: e.scalar_tensor_tensor(
                    att[r0_:r0_ + 64, og, :], pav[0:64, 0:TB], vtot[0:64, h, 0:1],
                    denb[:], ALU.add, ALU.mult))
            atv = p.mark("ve")
            for og in range(G):
                bank = ps[og % 2]
                mm = None
                for kg in range(G):
                    mm = p.op("tensor", lambda e, og=og, kg=kg, bank=bank: e.matmul(
                        bank[:, 0:TB], wos[:, kg, og * 128:(og + 1) * 128],
                        att[:, kg, :], start=(kg == 0), stop=(kg == G - 1)),
                        waits=[("ve", atv), "se"] if kg == 0 else [], inc="te")
                p.op("vector", lambda e, og=og, bank=bank: e.tensor_add(
                    xb[:, og, 0:TB], bank[:, 0:TB], xb[:, og, 0:TB]),
                    waits=[("te", mm)])
            emit_ln_fm(p, xb, xb, cm, TB)
            store_tok = p.dma("sync", fm_src(x_out, slice(q0, q0 + TB)), xb[:, :, 0:TB],
                              waits=["ve"])
        for qn in list(p.dqueue):
            p.dma_flush(qn)
        p.finish()
    return nc


def build_l1_attn(debug=False):
    nc = bass.Bass("TRN2", target_bir_lowering=False)
    x_in = nc.dram_tensor("x_in", [D, T], F32, kind="ExternalInput").ap()
    nw = nc.dram_tensor("nw", [D, 1], F32, kind="ExternalInput").ap()
    w_in = nc.dram_tensor("w_in", [D, 4096], F32, kind="ExternalInput").ap()
    aw = nc.dram_tensor("aw", [2048, 2048], F32, kind="ExternalInput").ap()
    ab_ = nc.dram_tensor("ab_", [2048, 1], F32, kind="ExternalInput").ap()
    abase = nc.dram_tensor("abase", [16, 1], F32, kind="ExternalInput").ap()
    bw = nc.dram_tensor("bw", [2048, 16], F32, kind="ExternalInput").ap()
    cw = nc.dram_tensor("cw", [2048, 16], F32, kind="ExternalInput").ap()
    dvec = nc.dram_tensor("dvec", [2048, 1], F32, kind="ExternalInput").ap()
    ow = nc.dram_tensor("ow", [2048, D], F32, kind="ExternalInput").ap()
    x_out = nc.dram_tensor("x_out", [D, T], F32, kind="ExternalOutput").ap()
    rehbm = nc.dram_tensor("rehbm", [16, 2048], BF16, kind="Internal").ap()
    if debug:
        dbg = {nm: nc.dram_tensor(nm, shp, F32, kind="ExternalOutput").ap()
               for nm, shp in (("d_xin", [128, 512]), ("d_z", [128, 512]),
                               ("d_rp", [128, 512]), ("d_bf", [16, 512]),
                               ("d_ws", [128, 64]), ("d_ys", [128, 512]),
                               ("d_xtm", [128, 2048]), ("d_h", [16, 2048]), ("d_tri", [32, 32]), ("d_w64", [32, 32]))}

    DIN = 2048
    GI = DIN // 128
    L = 32
    NCH = TB // L

    with contextlib.ExitStack() as ctx:
        p = P(nc, ctx)
        p.ones_row = p.sbuf("p_ones_row", [1, 128], F32)
        p.op("vector", lambda e: e.memset(p.ones_row[:], 1.0))
        onesr16 = p.sbuf("onesr16", [1, 16], F32)
        p.op("vector", lambda e: e.memset(onesr16[:], 1.0))
        identh = p.sbuf("identh", [128, 128], BF16)
        identf = p.sbuf("identf", [128, 128], F32)
        i1 = p.dma("sync", identh[:], nc.inline_tensor(np.eye(128, dtype=ml_dtypes.bfloat16), "idh1").ap())
        i2 = p.dma("sync", identf[:], nc.inline_tensor(np.eye(128, dtype=np.float32), "idf1").ap())
        p.wait_only("tensor", [i1, i2])

        sow = p.sbuf("sow", [128, GI, D], BF16)
        sbw = p.sbuf("sbw", [128, GI, 16], BF16)
        scw = p.sbuf("scw", [128, GI, 16], BF16)
        t1_ = p.dma("gpsimd", sow[:], ow.rearrange("(g p) c -> p g c", p=128))
        t2_ = p.dma("gpsimd", sbw[:], bw.rearrange("(g p) c -> p g c", p=128))
        t3_ = p.dma("gpsimd", scw[:], cw.rearrange("(g p) c -> p g c", p=128))
        snw = p.sbuf("snw", [128, G], F32)
        sab = p.sbuf("sab", [128, GI], F32)
        sd = p.sbuf("sd", [128, GI], F32)
        t4_ = p.dma("sync", snw[:, :], nw.rearrange("(g p) c -> p (g c)", p=128),
                    allow_slow_non_contiguous=True)
        t5_ = p.dma("sync", sab[:, :], ab_.rearrange("(g p) c -> p (g c)", p=128),
                    allow_slow_non_contiguous=True)
        t6_ = p.dma("sync", sd[:, :], dvec.rearrange("(g p) c -> p (g c)", p=128),
                    allow_slow_non_contiguous=True)
        sabase = p.sbuf("sabase", [16, 1], F32)
        t7_ = p.dma("sync", sabase[:], abase)
        p.wait_only("scalar", [t4_, t5_, t6_, t7_])
        p.wait_only("vector", [t4_, t5_, t6_, t7_])
        p.wait_only("tensor", [t1_, t2_, t3_])

        pp = [p.psum(f"pp{i}", [128, 512], F32) for i in range(2)]
        pc = p.psum("pc", [128, 1024], F32)
        pupd = p.psum("pupd", [16, 512], F32)
        pbc2 = p.psum("pbc2", [16, 512], F32)
        ptf = p.psum("ptf", [128, 128], F32)
        pth = p.psum("pth", [128, 128], BF16)

        lab = p.sbuf("lab", [16, 1], F32)
        ftile = p.sbuf("ftile", [16, 4, L], F32)
        fiota = p.sbuf("fiota", [1, L], F32)
        fexp = p.sbuf("fexp", [16, L], F32)
        ablrow = p.sbuf("ablrow", [1, 16], F32)
        ablcol = p.sbuf("ablcol", [16, 1], F32)
        p.op("gpsimd", lambda e: e.iota(fiota[:], [[1, L]], base=0, channel_multiplier=0,
                                        allow_small_or_imprecise_dtypes=True))
        p.op("scalar", lambda e: e.activation(lab[:], sabase[:], AF.Sigmoid))
        p.op("scalar", lambda e: e.activation(lab[:], lab[:], AF.Ln), waits=["se"])
        p.op("vector", lambda e: e.tensor_scalar_max(lab[:], lab[:], -5.4), waits=["se"])
        fmb = p.op("tensor", lambda e: e.matmul(pbc2[:, 0:L], onesr16[0:1, :], fiota[0:1, :],
                                                start=True, stop=True),
                   waits=["ve", "ge"], inc="te")
        p.op("vector", lambda e: e.tensor_copy(fexp[:], pbc2[0:16, 0:L]), waits=[("te", fmb)])
        for fi, (sc, sh) in enumerate(((1.0, -16.0), (-1.0, 16.0), (1.0, 1.0), (-1.0, float(L - 1)))):
            p.op("vector", lambda e, fi=fi, sc=sc, sh=sh: e.tensor_scalar(
                ftile[:, fi, :], fexp[:], sc, sh, ALU.mult, ALU.add))
            p.op("vector", lambda e, fi=fi: e.tensor_scalar_mul(
                ftile[:, fi, :], ftile[:, fi, :], lab[:, 0:1]))
            p.op("vector", lambda e, fi=fi: e.tensor_scalar_min(
                ftile[:, fi, :], ftile[:, fi, :], 86.0))
            p.op("vector", lambda e, fi=fi: e.tensor_scalar_max(
                ftile[:, fi, :], ftile[:, fi, :], -86.0))
            p.op("scalar", lambda e, fi=fi: e.activation(
                ftile[:, fi, :], ftile[:, fi, :], AF.Exp), waits=["ve"])
        p.op("vector", lambda e: e.tensor_scalar_mul(fexp[:, 0:1], lab[:, 0:1], float(L)),
             waits=["se"])
        p.op("vector", lambda e: e.tensor_scalar_min(fexp[:, 0:1], fexp[:, 0:1], 86.0))
        p.op("vector", lambda e: e.tensor_scalar_max(fexp[:, 0:1], fexp[:, 0:1], -86.0))
        p.op("scalar", lambda e: e.activation(fexp[:, 0:1], fexp[:, 0:1], AF.Exp),
             waits=["ve"])
        abv = p.mark("se")
        p.op("vector", lambda e: e.tensor_copy(ablcol[:], fexp[0:16, 0:1]),
             waits=[("se", abv)])

        tri = nc.inline_tensor(np.triu(np.ones((L, L), np.float32)), "triu64").ap()
        stri = p.sbuf("stri", [L, L], F32)
        stritok = p.dma("sync", stri[:], tri)
        p.wait_only("vector", [stritok])
        rstn = np.ones((128, TB), np.float32)
        rstn[:, ::L] = 0.0
        srst = p.sbuf("srst", [128, TB], F32)
        rsttok = p.dma("sync", srst[:], nc.inline_tensor(rstn, "rst").ap())
        p.wait_only("vector", [rsttok])

        xb = p.sbuf("xb", [128, G, TB], F32)
        xin = p.sbuf("xin", [128, GI, TB], BF16)
        zs = p.sbuf("zs", [128, GI, TB], BF16)
        rpre = p.sbuf("rpre", [128, GI, TB], F32)
        xtm = p.sbuf("xtm", [128, 4, DIN], BF16)
        yscan = p.sbuf("yscan", [128, GI, TB], BF16)
        xnh = yscan[:, 0:G, :]
        hst = p.sbuf("hst", [16, DIN], F32)
        bf_ = p.sbuf("bf_", [16, TB], F32)
        cf_ = p.sbuf("cf_", [16, TB], F32)
        restage = p.sbuf("restage", [16, DIN], BF16)
        resl = p.sbuf("resl", [1, DIN], F32)
        wst = [p.sbuf(f"wst{i}", [128, G, 128], BF16) for i in range(2)]
        ast = [p.sbuf(f"ast{i}", [128, GI, 128], BF16) for i in range(2)]
        tmpf = p.sbuf("tmpf", [128, TB], F32)
        tmp2 = p.sbuf("tmp2", [128, TB], F32)
        xth = p.sbuf("xth", [128, TB], BF16)
        chat = p.sbuf("chat", [16, L], F32)
        bhat = p.sbuf("bhat", [16, L], F32)
        btm = p.sbuf("btm", [96, 16], BF16)
        wsb = p.sbuf("wsb", [96, L], BF16)
        w64 = p.sbuf("w64", [L, L], F32)
        stg32 = p.sbuf("stg32", [32, DIN], BF16)
        p.shared_sq = tmp2
        p.shared_prow = pc[0:1, 0:1024]
        p.shared_pbc = pc
        cm = make_common(p, TB, slim=True)
        p.op("vector", lambda e: e.memset(hst[:], 0.0))

        store_tok = None
        for ib in range(NTB):
            t0 = ib * TB
            xld = p.dma("sync", xb[:], fm_src(x_in, slice(t0, t0 + TB)),
                        waits=(["ve", "se", "te"] + ([store_tok] if store_tok else [])))
            p.wait_only("scalar", [xld])
            for g in range(G):
                sv = p.op("scalar", lambda e, g=g: e.activation(
                    tmpf[:, :], xb[:, g, :], AF.Square), waits=[xld, "ve", "te"])
                p.op("tensor", lambda e, g=g: e.matmul(
                    pupd[0:1, 0:TB], cm["onesc"][:, 0:1], tmpf[:, :],
                    start=(g == 0), stop=(g == G - 1)), waits=[("se", sv)], inc="te")
            tv = p.mark("te")
            p.op("vector", lambda e: e.tensor_scalar(
                cm["r0"][0:1, :], pupd[0:1, 0:TB], 1.0 / D, 1e-6, ALU.mult, ALU.add),
                waits=[("te", tv)])
            p.op("scalar", lambda e: e.activation(cm["r0"][0:1, :], cm["r0"][0:1, :],
                                                  AF.Sqrt), waits=["ve"])
            p.op("vector", lambda e: e.reciprocal(cm["r0"][0:1, :], cm["r0"][0:1, :]),
                 waits=["se"])
            rv = p.mark("ve")
            bmm = p.op("tensor", lambda e: e.matmul(
                pc[:, 0:TB], p.ones_row[0:1, :], cm["r0"][0:1, 0:TB],
                start=True, stop=True), waits=[("ve", rv)], inc="te")
            for g in range(G):
                p.op("vector", lambda e, g=g: e.tensor_tensor(
                    tmpf[:, :], xb[:, g, :], pc[:, 0:TB], ALU.mult),
                    waits=[("te", bmm)])
                p.op("vector", lambda e, g=g: e.tensor_scalar_mul(
                    xnh[:, g, :], tmpf[:, :], snw[:, g:g + 1]))
            xnv = p.mark("ve")
            wtok = [None, None]
            for og in range(32):
                buf = wst[og % 2]
                wtok[og % 2] = p.dma("gpsimd", buf[:],
                                     w_in[:, og * 128:(og + 1) * 128].rearrange("(g p) c -> p g c", p=128),
                                     waits=["te"])
                bank = pp[og % 2]
                mm = None
                for kg in range(G):
                    mm = p.op("tensor", lambda e, kg=kg, bank=bank, buf=buf: e.matmul(
                        bank[:, 0:TB], buf[:, kg, :], xnh[:, kg, :],
                        start=(kg == 0), stop=(kg == G - 1)),
                        waits=[wtok[og % 2], ("ve", xnv), "se"] if kg == 0 else [], inc="te")
                dst = xin if og < 16 else zs
                p.op("scalar", lambda e, og=og, bank=bank, dst=dst: e.activation(
                    dst[:, og % 16, :], bank[:, 0:TB], AF.Silu), waits=[("te", mm)])
            xiv = p.mark("se")
            for wt, dstf in ((sbw, bf_), (scw, cf_)):
                mm = None
                for kg in range(GI):
                    mm = p.op("tensor", lambda e, kg=kg, wt=wt: e.matmul(
                        pupd[0:16, 0:TB], wt[:, kg, :], xin[:, kg, :],
                        start=(kg == 0), stop=(kg == GI - 1)),
                        waits=[("se", xiv), "ve"] if kg == 0 else [], inc="te")
                p.op("vector", lambda e, dstf=dstf, mm=mm: e.tensor_copy(
                    dstf[:, :], pupd[0:16, 0:TB]), waits=[("te", mm)])
            atok = [None, None]
            for g in range(GI):
                buf = ast[g % 2]
                atok[g % 2] = p.dma("gpsimd", buf[:],
                                    aw[:, g * 128:(g + 1) * 128].rearrange("(k p) c -> p k c", p=128),
                                    waits=["te"])
                bank = pp[g % 2]
                mm = None
                for kg in range(GI):
                    mm = p.op("tensor", lambda e, kg=kg, bank=bank, buf=buf: e.matmul(
                        bank[:, 0:TB], buf[:, kg, :], xin[:, kg, :],
                        start=(kg == 0), stop=(kg == GI - 1)),
                        waits=[atok[g % 2], "se", "ve"] if kg == 0 else [], inc="te")
                p.op("scalar", lambda e, g=g, bank=bank: e.activation(
                    tmpf[:, :], bank[:, 0:TB], AF.Sigmoid, bias=sab[:, g:g + 1]),
                    waits=[("te", mm), "ve"])
                p.op("scalar", lambda e: e.activation(tmpf[:, :], tmpf[:, :], AF.Ln),
                     waits=["se"])
                p.op("vector", lambda e: e.tensor_tensor_scan(
                    tmp2[:, :], srst[:, :], tmpf[:, :], 0.0, ALU.mult, ALU.add),
                    waits=["se"])
                p.op("vector", lambda e: e.tensor_scalar_max(tmp2[:, :], tmp2[:, :], -80.0))
                p.op("scalar", lambda e, g=g: e.activation(
                    rpre[:, g, :], tmp2[:, :], AF.Exp), waits=["ve"])
                p.op("scalar", lambda e: e.activation(
                    tmpf[:, :], tmp2[:, :], AF.Exp, scale=-1.0), waits=["se"])
                p.op("vector", lambda e, g=g: e.tensor_tensor(
                    xth[:, :], xin[:, g, :], tmpf[:, :], ALU.mult), waits=["se"])
                vv = p.mark("ve")
                for tt in range(4):
                    tmt = p.op("tensor", lambda e, tt=tt: e.transpose(
                        pth[:, :], xth[:, tt * 128:(tt + 1) * 128], identh[:]),
                        waits=[("ve", vv), "ve"], inc="te")
                    p.op("vector", lambda e, tt=tt, g=g: e.tensor_copy(
                        xtm[:, tt, g * 128:(g + 1) * 128], pth[:, :]), waits=[("te", tmt)])
                rv2 = p.mark("se")
                tmt = p.op("tensor", lambda e, g=g: e.transpose(
                    ptf[0:16, 0:128],
                    rpre[:, g, :].rearrange("p (c l) -> p c l", l=L)[:, :, L - 1:L].rearrange("p c l -> p (c l)"),
                    identf[:]), waits=[("se", rv2), "ve"], inc="te")
                p.op("vector", lambda e, g=g: e.tensor_copy(
                    restage[0:16, g * 128:(g + 1) * 128], ptf[0:16, 0:128]),
                    waits=[("te", tmt)])
            prepv = p.mark("ve")
            rd1 = p.dma("gpsimd", rehbm, restage[:], waits=[("ve", prepv)])
            if debug and ib == 0:
                dbf2 = tmp2
                p.dma("gpsimd", dbg["d_xin"].rearrange("p (a c) -> p a c", a=1),
                      xin[:, 0:1, :], waits=["ve", "se"])
                p.dma("gpsimd", dbg["d_z"].rearrange("p (a c) -> p a c", a=1),
                      zs[:, 0:1, :], waits=["ve", "se"])
                p.dma("sync", dbg["d_rp"].rearrange("p (a c) -> p a c", a=1),
                      rpre[:, 0:1, :], waits=["ve", "se"])
                p.dma("sync", dbg["d_bf"], bf_[:], waits=["ve", "se"])
                for qq in range(4):
                    p.op("vector", lambda e, qq=qq: e.tensor_copy(
                        dbf2[:, :], xtm[:, 0, qq * 512:(qq + 1) * 512]),
                        waits=["ve", "se", "te"])
                    p.dma("sync", dbg["d_xtm"][:, qq * 512:(qq + 1) * 512], dbf2[:, 0:512],
                          waits=["ve"])
            relast = [0]
            for ch in range(NCH):
                c0 = ch * L
                tt, rr = (c0 // 128), (c0 % 128)
                rld = p.dma("gpsimd", resl[:], rehbm[ch:ch + 1, :],
                            waits=[rd1, ("te", relast[0])])
                if rr == 96:
                    p.op("vector", lambda e, tt=tt: e.tensor_copy(
                        stg32[:, :], xtm[96:128, tt, :]), waits=["te"])
                    xsrc, rb = stg32, 0
                else:
                    xsrc, rb = None, rr
                p.op("vector", lambda e, ch=ch: e.tensor_tensor(
                    chat[:], cf_[:, ch * L:(ch + 1) * L], ftile[:, 0, :], ALU.mult),
                    waits=["te"])
                p.op("vector", lambda e, ch=ch: e.tensor_tensor(
                    bhat[:], bf_[:, ch * L:(ch + 1) * L], ftile[:, 1, :], ALU.mult))
                bcv = p.mark("ve")
                wmm = p.op("tensor", lambda e: e.matmul(
                    ptf[0:L, 0:L], bhat[:], chat[:], start=True, stop=True),
                    waits=[("ve", bcv)], inc="te")
                p.op("vector", lambda e: e.tensor_scalar_min(
                    w64[:, :], ptf[0:L, 0:L], 1.0e30), waits=[("te", wmm)])
                p.op("vector", lambda e: e.tensor_tensor(
                    w64[:, :], w64[:, :], stri[:], ALU.mult))
                for bb in range(3):
                    p.op("vector", lambda e, bb=bb: e.tensor_copy(
                        wsb[bb * 32:(bb + 1) * 32, :], w64[:, :]))
                p.op("vector", lambda e, ch=ch: e.tensor_tensor(
                    bhat[:], bf_[:, ch * L:(ch + 1) * L], ftile[:, 3, :], ALU.mult))
                bhv = p.mark("ve")
                tmt = p.op("tensor", lambda e: e.transpose(
                    ptf[0:L, 64:80], bhat[:], identf[0:16, 0:16]),
                    waits=[("ve", bhv), "ve"], inc="te")
                for bb in range(3):
                    p.op("vector", lambda e, bb=bb: e.tensor_copy(
                        btm[bb * 32:(bb + 1) * 32, :], ptf[0:L, 64:80]),
                        waits=[("te", tmt)] if bb == 0 else [])
                p.op("vector", lambda e, ch=ch: e.tensor_tensor(
                    chat[:], cf_[:, ch * L:(ch + 1) * L], ftile[:, 2, :], ALU.mult))
                inv_ = p.mark("ve")
                for dg in range(GI):
                    if xsrc is None:
                        p.op("tensor", lambda e, dg=dg, tt=tt, rr=rr: e.matmul(
                            pc[:, dg * 32:(dg + 1) * 32],
                            xtm[rr:rr + L, tt, dg * 128:(dg + 1) * 128],
                            wsb[rr:rr + L, :], start=(dg == 0), stop=False),
                            waits=[("ve", inv_), "se"] if dg == 0 else [], inc="te")
                    else:
                        p.op("tensor", lambda e, dg=dg: e.matmul(
                            pc[:, dg * 32:(dg + 1) * 32],
                            stg32[:, dg * 128:(dg + 1) * 128],
                            wsb[0:L, :], start=(dg == 0), stop=False),
                            waits=[("ve", inv_), "se", "ve"] if dg == 0 else [], inc="te")
                for dg in range(GI):
                    p.op("tensor", lambda e, dg=dg: e.matmul(
                        pc[:, dg * 32:(dg + 1) * 32], hst[:, dg * 128:(dg + 1) * 128],
                        chat[:], start=False, stop=(dg == GI - 1)), waits=[], inc="te")
                tvv = p.mark("te")
                p.op("vector", lambda e, c0=c0: e.tensor_tensor(
                    yscan[:, :, c0:c0 + L],
                    pc[:, 0:512].rearrange("p (a b) -> p a b", b=L),
                    rpre[:, :, c0:c0 + L], ALU.mult),
                    waits=[("te", tvv)])
                for q_ in range(4):
                    if xsrc is None:
                        um = p.op("tensor", lambda e, q_=q_, tt=tt, rr=rr: e.matmul(
                            pupd[:, :], btm[rr:rr + L, :],
                            xtm[rr:rr + L, tt, q_ * 512:(q_ + 1) * 512],
                            start=True, stop=True), waits=["ve"], inc="te")
                    else:
                        um = p.op("tensor", lambda e, q_=q_: e.matmul(
                            pupd[:, :], btm[0:L, :],
                            stg32[:, q_ * 512:(q_ + 1) * 512],
                            start=True, stop=True), waits=["ve"], inc="te")
                    bm = p.op("tensor", lambda e, q_=q_: e.matmul(
                        pbc2[:, :], onesr16[0:1, :], resl[0:1, q_ * 512:(q_ + 1) * 512],
                        start=True, stop=True), waits=[rld] if q_ == 0 else [], inc="te")
                    p.op("vector", lambda e, q_=q_: e.tensor_scalar_mul(
                        hst[:, q_ * 512:(q_ + 1) * 512], hst[:, q_ * 512:(q_ + 1) * 512],
                        ablcol[:, 0:1]))
                    p.op("vector", lambda e, q_=q_: e.tensor_tensor(
                        hst[:, q_ * 512:(q_ + 1) * 512], hst[:, q_ * 512:(q_ + 1) * 512],
                        pupd[0:16, :], ALU.add), waits=[("te", um)])
                    p.op("vector", lambda e, q_=q_: e.tensor_tensor(
                        hst[:, q_ * 512:(q_ + 1) * 512], hst[:, q_ * 512:(q_ + 1) * 512],
                        pbc2[0:16, :], ALU.mult), waits=[("te", bm)])
                    relast[0] = bm
            scv = p.mark("ve")
            if debug and ib == 0:
                tk1 = p.dma("gpsimd", dbg["d_ws"][0:32, 0:32].rearrange("p (a c) -> p a c", a=1),
                            wsb[0:32, :].rearrange("p (a c) -> p a c", a=1),
                            waits=["ve", "se", "te"])
                tk2 = p.dma("sync", dbg["d_tri"], stri[:], waits=["ve", "se", "te"])
                tk3 = p.dma("sync", dbg["d_w64"], w64[:], waits=["ve", "se", "te"])
                tk4 = p.dma("gpsimd", dbg["d_ys"].rearrange("p (a c) -> p a c", a=1),
                            yscan[:, 0:1, :], waits=["ve", "se", "te"])
                tk5 = p.dma("sync", dbg["d_h"], hst[:], waits=["ve", "se", "te"])
                p.wait_only("vector", [tk1, tk2, tk3, tk4, tk5])
                p.wait_only("scalar", [tk1, tk2, tk3, tk4, tk5])
            for g in range(GI):
                p.op("vector", lambda e, g=g: e.scalar_tensor_tensor(
                    tmpf[:, :], xin[:, g, :], sd[:, g:g + 1], yscan[:, g, :],
                    ALU.mult, ALU.add), waits=[("ve", scv)])
                p.op("vector", lambda e, g=g: e.tensor_tensor(
                    xin[:, g, :], tmpf[:, :], zs[:, g, :], ALU.mult))
            epv = p.mark("ve")
            for og in range(G):
                bank = pp[og % 2]
                mm = None
                for kg in range(GI):
                    mm = p.op("tensor", lambda e, og=og, kg=kg, bank=bank: e.matmul(
                        bank[:, 0:TB], sow[:, kg, og * 128:(og + 1) * 128],
                        xin[:, kg, :], start=(kg == 0), stop=(kg == GI - 1)),
                        waits=[("ve", epv), "se"] if kg == 0 else [], inc="te")
                p.op("vector", lambda e, og=og, bank=bank: e.scalar_tensor_tensor(
                    xb[:, og, :], xb[:, og, :], 2.0, bank[:, 0:TB],
                    ALU.mult, ALU.add), waits=[("te", mm)])
            emit_ln_fm(p, xb, xb, cm, TB)
            store_tok = p.dma("sync", fm_src(x_out, slice(t0, t0 + TB)), xb[:],
                              waits=["ve"])
        for qn in list(p.dqueue):
            p.dma_flush(qn)
        p.finish()
    return nc


def _l3a_terms():
    BS = [4, 8, 16, 32]
    RK = [4, 8, 8, 8]
    NN = [8, 4, 2, 1]
    terms = []
    for n in range(16):
        terms.append(dict(kind="leaf", tile=0, row0=4 * n, r=4, lo=4 * n, bs=4))
    row = 64
    t2row = {0: 64, 1: 96}
    for l in range(4):
        for m in range(NN[l]):
            tile = 0 if l <= 1 else 1
            r0 = {0: 64 + 4 * m, 1: 96 + 8 * m, 2: 0 + 8 * m, 3: 16 + 8 * m}[l]
            terms.append(dict(kind="top", tile=tile, row0=r0, r=RK[l], l=l, m=m,
                              bs=BS[l], lo=2 * BS[l] * m))
    for l in range(4):
        for m in range(NN[l]):
            r0 = {0: 24 + 4 * m, 1: 56 + 8 * m, 2: 88 + 8 * m, 3: 104 + 8 * m}[l]
            terms.append(dict(kind="bot", tile=1, row0=r0, r=RK[l], l=l, m=m,
                              bs=BS[l], lo=2 * BS[l] * m))
    return terms


def build_l3_attn(debug=False):
    nc = bass.Bass("TRN2", target_bir_lowering=False)
    x_in = nc.dram_tensor("x_in", [D, T], F32, kind="ExternalInput").ap()
    wq = nc.dram_tensor("wq", [D, 64], F32, kind="ExternalInput").ap()
    wk = nc.dram_tensor("wk", [D, 64], F32, kind="ExternalInput").ap()
    wv = nc.dram_tensor("wv", [D, 64], F32, kind="ExternalInput").ap()
    wo = nc.dram_tensor("wo", [64, D], F32, kind="ExternalInput").ap()
    uvm = {}
    BS = [4, 8, 16, 32]
    RK = [4, 8, 8, 8]
    for nm in ("UL", "VR", "UR", "VL"):
        for l in range(4):
            uvm[(nm, l)] = nc.dram_tensor(f"{nm}{l}", [BS[l], RK[l]], F32,
                                          kind="ExternalInput").ap()
    x_out = nc.dram_tensor("x_out", [D, T], F32, kind="ExternalOutput").ap()
    psi_img = nc.dram_tensor("psi_img", [64, 240], F32, kind="Internal").ap()
    phv_img = nc.dram_tensor("phv_img", [64, 240], F32, kind="Internal").ap()
    phk_img = nc.dram_tensor("phk_img", [64, 240], F32, kind="Internal").ap()
    rk_img = nc.dram_tensor("rk_img", [64, 46 * 64], F32, kind="Internal").ap()
    rv_img = nc.dram_tensor("rv_img", [64, 46 * 64], F32, kind="Internal").ap()

    terms = _l3a_terms()
    L = 128
    NCHK = T // L

    with contextlib.ExitStack() as ctx:
        p = P(nc, ctx)
        p.ones_row = p.sbuf("p_ones_row", [1, 128], F32)
        p.op("vector", lambda e: e.memset(p.ones_row[:], 1.0))
        onescolh = p.sbuf("onescolh", [128, 1], BF16)
        p.op("vector", lambda e: e.memset(onescolh[:], 1.0))
        identf = p.sbuf("identf", [128, 128], F32)
        i2 = p.dma("sync", identf[:], nc.inline_tensor(np.eye(128, dtype=np.float32), "idf2").ap())
        p.wait_only("tensor", [i2])

        zstage = p.sbuf("zstage", [64, 46 * 64], F32)
        p.op("vector", lambda e: e.memset(zstage[:], 0.0))
        zv = p.mark("ve")
        toks = []
        for img, wcols in ((psi_img, 240), (phv_img, 240), (phk_img, 240),
                           (rk_img, 46 * 64), (rv_img, 46 * 64)):
            toks.append(p.dma("sync", img, zstage[0:64, 0:wcols], waits=[("ve", zv)]))
        eye4 = nc.inline_tensor(np.eye(4, dtype=np.float32), "eye4i").ap()
        uvs = p.sbuf("uvs", [32, 4 * 8], F32)
        uvt = p.sbuf("uvt", [8, 4 * 32], F32)
        gst = p.sbuf("gst", [32, 4 * 64], F32)
        pg_ = p.psum("pg_", [32, 320], F32)
        g_toks = {}
        for l in range(4):
            bs, r = BS[l], RK[l]
            ltoks = []
            for j, nm in enumerate(("UL", "VR", "UR", "VL")):
                tk = p.dma("sync", uvs[0:bs, j * 8:j * 8 + r], uvm[(nm, l)],
                           waits=["te", "ve"])
                ltoks.append(tk)
            for j in range(4):
                tmt = p.op("tensor", lambda e, j=j, bs=bs, r=r: e.transpose(
                    pg_[0:r, j * 40:j * 40 + bs], uvs[0:bs, j * 8:j * 8 + r],
                    identf[0:bs, 0:bs]), waits=ltoks + ["ve"], inc="te")
                p.op("vector", lambda e, j=j, bs=bs, r=r, tmt=tmt: e.tensor_copy(
                    uvt[0:r, j * 32:j * 32 + bs], pg_[0:r, j * 40:j * 40 + bs]),
                    waits=[("te", tmt)])
            uv = p.mark("ve")
            m1 = p.op("tensor", lambda e, bs=bs, r=r: e.matmul(
                pg_[0:bs, 160:160 + bs], uvt[0:r, 0:bs], uvt[0:r, 0:bs],
                start=True, stop=True), waits=[("ve", uv)], inc="te")
            m2 = p.op("tensor", lambda e, bs=bs, r=r: e.matmul(
                pg_[0:bs, 224:224 + bs], uvt[0:r, 32:32 + bs], uvt[0:r, 64:64 + bs],
                start=True, stop=True), waits=[], inc="te")
            p.op("vector", lambda e, bs=bs, l=l, m2=m2: e.tensor_copy(
                gst[0:bs, l * 64:l * 64 + bs], pg_[0:bs, 160:160 + bs]),
                waits=[("te", m2)])
            p.op("vector", lambda e, bs=bs, l=l: e.tensor_copy(
                gst[0:bs, l * 64 + 32:l * 64 + 32 + bs], pg_[0:bs, 224:224 + bs]))
            g_toks[l] = p.mark("ve")
        asm = []
        for ti, t_ in enumerate(terms):
            i64 = ti * 64
            if t_["kind"] == "leaf":
                lo = t_["lo"]
                asm.append(p.dma("sync", psi_img[lo:lo + 4, ti * 4 if False else 0:0], None) if False else None)
                p.dma("sync", psi_img[lo:lo + 4, t_["row0"]:t_["row0"] + 4], eye4,
                      waits=[toks[0]])
                p.dma("sync", phv_img[lo:lo + 4, t_["row0"]:t_["row0"] + 4], eye4,
                      waits=[toks[1]])
                p.dma("sync", rk_img[lo:lo + 4, i64 + lo:i64 + lo + 4], eye4,
                      waits=[toks[3]])
            elif t_["kind"] == "top":
                l, bs, r, lo = t_["l"], t_["bs"], t_["r"], t_["lo"]
                hi = lo + bs
                I0 = t_["row0"] + (0 if t_["tile"] == 0 else 128)
                p.dma("sync", psi_img[hi:hi + bs, I0:I0 + r], uvm[("VR", l)],
                      waits=[toks[0]])
                p.dma("sync", phv_img[hi:hi + bs, I0:I0 + r], uvm[("VR", l)],
                      waits=[toks[1]])
                p.dma("sync", rk_img[lo:lo + bs, i64 + lo:i64 + lo + bs],
                      gst[0:bs, l * 64:l * 64 + bs],
                      waits=[toks[3], ("ve", g_toks[l])])
            else:
                l, bs, r, lo = t_["l"], t_["bs"], t_["r"], t_["lo"]
                hi = lo + bs
                I0 = t_["row0"] + 128
                p.dma("sync", psi_img[lo:lo + bs, I0:I0 + r], uvm[("VL", l)],
                      waits=[toks[0]])
                p.dma("sync", phk_img[lo:lo + bs, I0:I0 + r], uvm[("UL", l)],
                      waits=[toks[2]])
                p.dma("sync", rv_img[hi:hi + bs, i64 + hi:i64 + hi + bs],
                      gst[0:bs, l * 64 + 32:l * 64 + 32 + bs],
                      waits=[toks[4], ("ve", g_toks[l])])
        spsi = p.sbuf("spsi", [64, 240], BF16)
        sphv = p.sbuf("sphv", [64, 240], BF16)
        sphk = p.sbuf("sphk", [64, 240], BF16)
        srk = p.sbuf("srk", [64, 46 * 64], BF16)
        srv = p.sbuf("srv", [64, 46 * 64], BF16)
        ldt = []
        asm_waits = [("Dsync0", p.tot.get("Dsync0", 0)), ("Dsync1", p.tot.get("Dsync1", 0))]
        for dst, img in ((spsi, psi_img), (sphv, phv_img), (sphk, phk_img),
                         (srk, rk_img), (srv, rv_img)):
            ldt.append(p.dma("gpsimd", dst[:], img, waits=asm_waits))
        p.wait_only("tensor", ldt)
        if debug:
            imgdbg = nc.dram_tensor("imgdbg", [64, 3 * 240 + 2 * 2944], F32,
                                    kind="ExternalOutput").ap()
            off = 0
            for dst, w_ in ((spsi, 240), (sphv, 240), (sphk, 240),
                            (srk, 2944), (srv, 2944)):
                p.dma("gpsimd", imgdbg[:, off:off + w_], dst[:], waits=ldt)
                off += w_
        swq = p.sbuf("swq", [128, G, 64], BF16)
        swk = p.sbuf("swk", [128, G, 64], BF16)
        swv = p.sbuf("swv", [128, G, 64], BF16)
        swo = p.sbuf("swo", [64, D], BF16)
        wt1 = p.dma("gpsimd", swq[:], wq.rearrange("(g p) c -> p g c", p=128))
        wt2 = p.dma("gpsimd", swk[:], wk.rearrange("(g p) c -> p g c", p=128))
        wt3 = p.dma("gpsimd", swv[:], wv.rearrange("(g p) c -> p g c", p=128))
        wt4 = p.dma("gpsimd", swo[:], wo)
        p.wait_only("tensor", [wt1, wt2, wt3, wt4])
        tm1 = np.zeros((128, 46), np.float32)
        tm2 = np.zeros((112, 46), np.float32)
        for ti, t_ in enumerate(terms):
            if t_["tile"] == 0:
                tm1[t_["row0"]:t_["row0"] + t_["r"], ti] = 1.0
            else:
                tm2[t_["row0"]:t_["row0"] + t_["r"], ti] = 1.0
        stm1 = p.sbuf("stm1", [128, 46], F32)
        stm2 = p.sbuf("stm2", [112, 46], F32)
        mk1 = p.dma("sync", stm1[:], nc.inline_tensor(tm1, "tmask1").ap())
        mk2 = p.dma("sync", stm2[:], nc.inline_tensor(tm2, "tmask2").ap())
        striu = p.sbuf("striu", [128, 128], F32)
        mk3 = p.dma("sync", striu[:], nc.inline_tensor(
            np.triu(np.ones((128, 128), np.float32)), "triu128").ap())
        p.wait_only("vector", [mk1, mk2, mk3])

        pAb = p.psum("pAb", [128, 1024], F32)
        pA = [pAb[:, 0:512], pAb[:, 512:1024]]
        pbig = p.psum("pbigL3", [128, 2048], F32)
        pT = pbig[:, 0:128]
        py = pbig[0:64, 512:640]
        pG = pbig[:, 1024:1152]
        pden = pbig[0:1, 1152:1280]
        pbcst = pbig[0:64, 1280:1408]
        pMs = pbig[0:64, 1536:1776]

        Ms = p.sbuf("Ms", [64, 240], F32)
        Mt1 = p.sbuf("Mt1", [128, 64], BF16)
        Mt2 = p.sbuf("Mt2", [112, 64], BF16)
        zc = p.sbuf("zc", [64, 1], F32)
        p.op("vector", lambda e: e.memset(Ms[:], 0.0))
        p.op("vector", lambda e: e.memset(Mt1[:], 0.0))
        p.op("vector", lambda e: e.memset(Mt2[:], 0.0))
        p.op("vector", lambda e: e.memset(zc[:], 0.0))

        xb = p.sbuf("xb", [128, G, L], F32)
        xbh = p.sbuf("xbh", [128, G, L], BF16)
        qf = p.sbuf("qf", [64, L], BF16)
        kf = p.sbuf("kf", [64, L], BF16)
        vf = p.sbuf("vf", [64, L], BF16)
        e1 = p.sbuf("e1", [64, L], F32)
        e2 = p.sbuf("e2", [64, L], F32)
        psif = [p.sbuf(f"psif{i}", [128, L], BF16) for i in range(2)]
        phif = [p.sbuf(f"phif{i}", [128, L], BF16) for i in range(2)]
        phitm = p.sbuf("phitm", [128, 240], BF16)
        ustk = p.sbuf("ustk", [128, 46 * 64], BF16)
        psmk = p.sbuf("psmk", [128, L], BF16)
        tsb = p.sbuf("tsb", [128, L], BF16)
        gsb = p.sbuf("gsb", [128, L], BF16)
        yall = p.sbuf("yall", [64, T], BF16)
        denr = p.sbuf("denr", [1, L], F32)
        denbs = p.sbuf("denbs", [64, L], F32)
        yf = p.sbuf("yf", [64, L], F32)

        for chk in range(NCHK):
            c0 = chk * L
            xld = p.dma("sync", xb[:], fm_src(x_in, slice(c0, c0 + L)),
                        waits=["ve", "se", "te"])
            p.wait_only("vector", [xld])
            for g in range(G):
                p.op("vector", lambda e, g=g: e.tensor_copy(xbh[:, g, :], xb[:, g, :]))
            xv = p.mark("ve")
            for wmat, dst, elu in ((swq, qf, True), (swk, kf, True), (swv, vf, False)):
                mm = None
                for kg in range(G):
                    mm = p.op("tensor", lambda e, kg=kg, wmat=wmat: e.matmul(
                        pA[0][0:64, 0:L], wmat[:, kg, :], xbh[:, kg, :],
                        start=(kg == 0), stop=(kg == G - 1)),
                        waits=[("ve", xv), "se", "ve"] if kg == 0 else [], inc="te")
                if elu:
                    p.op("vector", lambda e: e.tensor_scalar_min(e1[:], pA[0][0:64, 0:L], 0.0),
                         waits=[("te", mm)])
                    p.op("vector", lambda e: e.tensor_scalar_max(e2[:], pA[0][0:64, 0:L], 0.0))
                    p.op("scalar", lambda e: e.activation(e1[:], e1[:], AF.Exp), waits=["ve"])
                    p.op("vector", lambda e, dst=dst: e.tensor_add(dst[:], e1[:], e2[:]),
                         waits=["se"])
                else:
                    p.op("vector", lambda e, dst=dst: e.tensor_copy(dst[:], pA[0][0:64, 0:L]),
                         waits=[("te", mm)])
            qkv = p.mark("ve")
            for tl, wid in ((0, 128), (1, 112)):
                mm = p.op("tensor", lambda e, tl=tl, wid=wid: e.matmul(
                    pT[0:wid, 0:L], spsi[:, tl * 128:tl * 128 + wid], qf[:],
                    start=True, stop=True),
                    waits=[("ve", qkv), "se", "ve"], inc="te")
                p.op("vector", lambda e, tl=tl, wid=wid, mm=mm: e.tensor_copy(
                    psif[tl][0:wid, :], pT[0:wid, 0:L]), waits=[("te", mm)])
            for tl, wid in ((0, 128), (1, 112)):
                p.op("tensor", lambda e, tl=tl, wid=wid: e.matmul(
                    pT[0:wid, 0:L], sphv[:, tl * 128:tl * 128 + wid], vf[:],
                    start=True, stop=False), waits=["ve", "se"], inc="te")
                mm = p.op("tensor", lambda e, tl=tl, wid=wid: e.matmul(
                    pT[0:wid, 0:L], sphk[:, tl * 128:tl * 128 + wid], kf[:],
                    start=False, stop=True), waits=[], inc="te")
                p.op("vector", lambda e, tl=tl, wid=wid, mm=mm: e.tensor_copy(
                    phif[tl][0:wid, :], pT[0:wid, 0:L]), waits=[("te", mm)])
            p.op("tensor", lambda e: e.matmul(
                pA[1][:, 0:240], vf[:], sphv[:, :], start=True, stop=False),
                waits=["ve", "se"], inc="te")
            mm = p.op("tensor", lambda e: e.matmul(
                pA[1][:, 0:240], kf[:], sphk[:, :], start=False, stop=True),
                waits=[], inc="te")
            p.op("vector", lambda e, mm=mm: e.tensor_copy(phitm[:, :], pA[1][:, 0:240]),
                 waits=[("te", mm)])
            for uc in range(6):
                uw = min(512, 46 * 64 - uc * 512)
                p.op("tensor", lambda e, uc=uc, uw=uw: e.matmul(
                    pA[uc % 2][:, 0:uw], kf[:], srk[:, uc * 512:uc * 512 + uw],
                    start=True, stop=False), waits=["ve", "se"], inc="te")
                mm = p.op("tensor", lambda e, uc=uc, uw=uw: e.matmul(
                    pA[uc % 2][:, 0:uw], vf[:], srv[:, uc * 512:uc * 512 + uw],
                    start=False, stop=True), waits=[], inc="te")
                p.op("vector", lambda e, uc=uc, uw=uw, mm=mm: e.tensor_copy(
                    ustk[:, uc * 512:uc * 512 + uw], pA[uc % 2][:, 0:uw]),
                    waits=[("te", mm)])
            uv2 = p.mark("ve")
            if debug and chk == 0:
                c0dbg = nc.dram_tensor("c0dbg", [128, 4400],
                                       F32, kind="ExternalOutput").ap()
                off = [0]
                def dmp(ap_, w_, part=128):
                    tk = p.dma("gpsimd", c0dbg[0:part, off[0]:off[0] + w_], ap_,
                               waits=[("ve", p.mark("ve")), "te", "se"])
                    p.wait_only("vector", [tk])
                    p.wait_only("tensor", [tk])
                    off[0] += w_
                dmp(qf[:].rearrange("a b -> a b"), 128, 64)
                dmp(kf[:], 128, 64)
                dmp(vf[:], 128, 64)
                dmp(psif[0][:, :], 128, 128)
                dmp(psif[1][0:112, :], 128, 112)
                dmp(phif[0][:, :], 128, 128)
                dmp(phif[1][0:112, :], 128, 112)
                dmp(phitm[:, :], 240)
                dmp(ustk[:, :], 2944)
                dmp(striu[:], 128)
                dmp(stm1[:, 0:46], 46)
                dmp(stm2[0:112, 0:46], 46, 112)
            nyt = 0
            for ti, t_ in enumerate(terms):
                tl = t_["tile"]
                wid = 128 if tl == 0 else 112
                p.op("vector", lambda e, ti=ti, tl=tl, wid=wid: e.tensor_scalar_mul(
                    psmk[0:wid, :], psif[tl][0:wid, :],
                    (stm1 if tl == 0 else stm2)[:, ti:ti + 1]), waits=["te"])
                pv = p.mark("ve")
                tmm = p.op("tensor", lambda e, tl=tl, wid=wid: e.matmul(
                    pT[:, 0:L], phif[tl][0:wid, :], psmk[0:wid, :],
                    start=True, stop=True), waits=[("ve", pv)], inc="te")
                p.op("vector", lambda e, tmm=tmm: e.tensor_tensor(
                    tsb[:, :], pT[:, 0:L], striu[:], ALU.mult), waits=[("te", tmm)])
                tv2 = p.mark("ve")
                p.op("tensor", lambda e, ti=ti, nyt=nyt: e.matmul(
                    py[:, 0:L], ustk[:, ti * 64:(ti + 1) * 64], tsb[:, :],
                    start=(nyt == 0), stop=False), waits=[("ve", tv2)], inc="te")
                nyt += 1
            p.op("tensor", lambda e: e.matmul(
                py[:, 0:L], Mt1[:, :], psif[0][:, :], start=False, stop=False),
                waits=[], inc="te")
            ylast = p.op("tensor", lambda e: e.matmul(
                py[:, 0:L], Mt2[0:112, :], psif[1][0:112, :], start=False, stop=True),
                waits=[], inc="te")
            gmm = p.op("tensor", lambda e: e.matmul(
                pG[:, 0:L], kf[:], qf[:], start=True, stop=True),
                waits=["ve", "se"], inc="te")
            p.op("vector", lambda e, gmm=gmm: e.tensor_tensor(
                gsb[:, :], pG[:, 0:L], striu[:], ALU.mult), waits=[("te", gmm)])
            gv = p.mark("ve")
            d1 = p.op("tensor", lambda e: e.matmul(
                pden[0:1, 0:L], onescolh[:, 0:1], gsb[:, :], start=True, stop=False),
                waits=[("ve", gv)], inc="te")
            zh = p.sbuf("zh", [64, 1], BF16) if chk == 0 else zh
            p.op("vector", lambda e: e.tensor_copy(zh[:], zc[:]), waits=["te"])
            zv2 = p.mark("ve")
            d2 = p.op("tensor", lambda e: e.matmul(
                pden[0:1, 0:L], zh[0:64, 0:1], qf[:], start=False, stop=True),
                waits=[("ve", zv2)], inc="te")
            p.op("vector", lambda e, d2=d2: e.tensor_scalar_max(
                denr[0:1, :], pden[0:1, 0:L], 1e-6), waits=[("te", d2)])
            p.op("vector", lambda e: e.reciprocal(denr[0:1, :], denr[0:1, :]))
            rv = p.mark("ve")
            dmm = p.op("tensor", lambda e: e.matmul(
                pbcst[:, 0:L], p.ones_row[0:1, 0:64], denr[0:1, 0:L],
                start=True, stop=True), waits=[("ve", rv)], inc="te")
            p.op("vector", lambda e, dmm=dmm: e.tensor_copy(denbs[:], pbcst[0:64, 0:L]),
                 waits=[("te", dmm)])
            if debug and chk == 0:
                p.op("vector", lambda e, ylast=ylast: e.tensor_copy(
                    yf[:, 0:64], py[0:64, 0:64]), waits=[("te", ylast)])
                p.dma("gpsimd", c0dbg[0:64, off[0]:off[0] + 64],
                      yf[:, 0:64], waits=[("ve", p.mark("ve"))])
                off[0] += 64
            p.op("vector", lambda e, ylast=ylast, c0=c0: e.tensor_tensor(
                yall[:, c0:c0 + L], py[0:64, 0:L], denbs[:], ALU.mult),
                waits=[("te", ylast)])
            nst = 0
            for ti, t_ in enumerate(terms):
                I0 = t_["row0"] + (0 if t_["tile"] == 0 else 128)
                r = t_["r"]
                p.op("tensor", lambda e, ti=ti, I0=I0, r=r, nst=nst: e.matmul(
                    pMs[:, I0:I0 + r], ustk[:, ti * 64:(ti + 1) * 64],
                    phitm[:, I0:I0 + r], start=(nst == 0), stop=(ti == len(terms) - 1)),
                    waits=(["ve"] if nst == 0 else []), inc="te")
                nst += 1
            msm = p.mark("te")
            p.op("vector", lambda e: e.tensor_tensor(Ms[:], Ms[:], pMs[0:64, 0:240],
                                                     ALU.add), waits=[("te", msm)])
            p.op("vector", lambda e: e.reduce_sum(e1[:, 0:1], kf[:], axis=AX.X))
            p.op("vector", lambda e: e.tensor_add(zc[:], zc[:], e1[:, 0:1]))
            msv = p.mark("ve")
            t1m = p.op("tensor", lambda e: e.transpose(
                pT[0:128, 0:64], Ms[:, 0:128], identf[0:64, 0:64]),
                waits=[("ve", msv), "ve"], inc="te")
            p.op("vector", lambda e, t1m=t1m: e.tensor_copy(Mt1[:, :], pT[0:128, 0:64]),
                 waits=[("te", t1m)])
            t2m = p.op("tensor", lambda e: e.transpose(
                pT[0:112, 0:64], Ms[:, 128:240], identf[0:64, 0:64]),
                waits=["ve"], inc="te")
            p.op("vector", lambda e, t2m=t2m: e.tensor_copy(Mt2[0:112, :], pT[0:112, 0:64]),
                 waits=[("te", t2m)])
        p.shared_sq = None
        p.shared_prow = pAb[0:1, 0:1024]
        p.shared_pbc = pAb
        cm = make_common(p, TB, slim=True)
        xb2 = p.sbuf("xb2", [128, G, TB], F32)
        store_tok = None
        for ib in range(NTB):
            t0 = ib * TB
            xld = p.dma("sync", xb2[:], fm_src(x_in, slice(t0, t0 + TB)),
                        waits=(["ve", "se", "te"] + ([store_tok] if store_tok else [])))
            p.wait_only("vector", [xld])
            for og in range(G):
                mm = p.op("tensor", lambda e, og=og, t0=t0: e.matmul(
                    pA[og % 2][:, 0:TB], swo[:, og * 128:(og + 1) * 128],
                    yall[:, t0:t0 + TB], start=True, stop=True),
                    waits=["ve", "se"], inc="te")
                p.op("vector", lambda e, og=og, mm=mm: e.tensor_add(
                    xb2[:, og, :], pA[og % 2][:, 0:TB], xb2[:, og, :]),
                    waits=[("te", mm), xld])
            emit_ln_fm(p, xb2, xb2, cm, TB)
            store_tok = p.dma("sync", fm_src(x_out, slice(t0, t0 + TB)), xb2[:],
                              waits=["ve"])
        for qn in list(p.dqueue):
            p.dma_flush(qn)
        p.finish()
    return nc


def build_l3_ffn(stop_after=None):
    nc = bass.Bass("TRN2", target_bir_lowering=False)
    x_in = nc.dram_tensor("x_in", [D, T], F32, kind="ExternalInput").ap()
    wqkv = nc.dram_tensor("wqkv", [6, D, 3072], F32, kind="ExternalInput").ap()
    qsc = nc.dram_tensor("qsc", [6, 64], F32, kind="ExternalInput").ap()
    ksc = nc.dram_tensor("ksc", [6, 64], F32, kind="ExternalInput").ap()
    wo6 = nc.dram_tensor("wo6", [6, D, D], F32, kind="ExternalInput").ap()
    lng = nc.dram_tensor("lng", [6, D], F32, kind="ExternalInput").ap()
    lnb = nc.dram_tensor("lnb", [6, D], F32, kind="ExternalInput").ap()
    w16 = nc.dram_tensor("w16", [6, D, 5460], F32, kind="ExternalInput").ap()
    b16 = nc.dram_tensor("b16", [6, 5460], F32, kind="ExternalInput").ap()
    w26 = nc.dram_tensor("w26", [6, 2730, D], F32, kind="ExternalInput").ap()
    b26 = nc.dram_tensor("b26", [6, D], F32, kind="ExternalInput").ap()
    x_out = nc.dram_tensor("x_out", [D, T], F32, kind="ExternalOutput").ap()
    xA = nc.dram_tensor("xA", [D, T], F32, kind="Internal").ap()
    xB = nc.dram_tensor("xB", [D, T], F32, kind="Internal").ap()
    mhbm3 = nc.dram_tensor("mhbm3", [64, 128], F32, kind="Internal").ap()

    NH, DH, DI, NAC = 16, 64, 2730, 22
    QKS = 8.0

    with contextlib.ExitStack() as ctx:
        p = P(nc, ctx)
        p.ones_row = p.sbuf("p_ones_row", [1, 128], F32)
        p.op("vector", lambda e: e.memset(p.ones_row[:], 1.0))
        onescol = p.sbuf("onescol3", [128, 1], F32)
        p.op("vector", lambda e: e.memset(onescol[:], 1.0))

        pbig3 = p.psum("pbig3", [128, 4096], F32)
        pfb = [pbig3[:, i * 512:(i + 1) * 512] for i in range(8)]
        ps2 = [pfb[0], pfb[1]]
        pav = [pfb[2][0:65, :], pfb[3][0:65, :]]
        ptab = pfb[4]
        pnr = pfb[5]
        p.shared_prow = pbig3[0:1, 3072:4096]

        half = np.arange(0, 64, 2, dtype=np.float64)
        invf_half = 1.0 / 10000.0 ** (half / 64.0)
        invf64 = np.concatenate([invf_half, invf_half])
        invf128 = np.concatenate([invf64, invf64]).reshape(128, 1).astype(np.float32)
        bsc = (half + 0.4 * 64) / (1.4 * 64)
        lsc64 = np.log(np.concatenate([bsc, bsc]))
        lsc128 = np.concatenate([lsc64, lsc64]).reshape(128, 1).astype(np.float32)
        sgn = np.ones((128, 1), np.float32)
        sgn[0:32] = -1.0
        sgn[64:96] = -1.0
        sinvf = p.sbuf("sinvf", [128, 1], F32)
        slsc = p.sbuf("slsc", [128, 1], F32)
        ssgn = p.sbuf("ssgn", [128, 1], F32)
        c1 = p.dma("sync", sinvf[:], nc.inline_tensor(invf128, "invf").ap())
        c2 = p.dma("sync", slsc[:], nc.inline_tensor(lsc128, "lsc").ap())
        c3 = p.dma("sync", ssgn[:], nc.inline_tensor(sgn, "sgn128").ap())
        p.wait_only("vector", [c1, c2, c3])
        p.wait_only("tensor", [c1, c2, c3])
        tcs = p.sbuf("tcs", [128, T], BF16)
        tss = p.sbuf("tss", [128, T], BF16)
        tci = p.sbuf("tci", [128, T], BF16)
        tsi = p.sbuf("tsi", [128, T], BF16)
        trow = p.sbuf("trow", [1, 512], F32)
        TWO_PI = float(2 * np.pi)
        RCC = 12582912.0
        for tb in range(8):
            t0 = tb * 512
            p.op("gpsimd", lambda e, t0=t0: e.iota(trow[:], [[1, 512]], base=t0,
                 channel_multiplier=0, allow_small_or_imprecise_dtypes=True),
                 waits=["te"])
            iv = p.mark("ge")
            mmt = p.op("tensor", lambda e: e.matmul(
                ptab[:, :], p.ones_row[0:1, :], trow[0:1, :], start=True, stop=True),
                waits=[("ge", iv), "ve", "se"], inc="te")
            p.op("vector", lambda e, mmt=mmt: e.tensor_scalar_mul(
                bsin, ptab[:, :], sinvf[:, 0:1]), waits=[("te", mmt)])
            p.op("vector", lambda e: e.tensor_scalar_mul(btmp, bsin, 1.0 / TWO_PI))
            p.op("vector", lambda e: e.tensor_scalar_add(btmp, btmp, RCC))
            p.op("vector", lambda e: e.tensor_scalar_sub(btmp, btmp, RCC))
            p.op("vector", lambda e: e.scalar_tensor_tensor(
                bsin, btmp, -TWO_PI, bsin, ALU.mult, ALU.add))
            p.op("vector", lambda e: e.tensor_scalar(
                bcos, bsin, -1.0, float(np.pi / 2), ALU.mult, ALU.add))
            p.op("vector", lambda e: e.tensor_scalar_mul(btmp, bcos, 1.0 / TWO_PI))
            p.op("vector", lambda e: e.tensor_scalar_add(btmp, btmp, RCC))
            p.op("vector", lambda e: e.tensor_scalar_sub(btmp, btmp, RCC))
            p.op("vector", lambda e: e.scalar_tensor_tensor(
                bcos, btmp, -TWO_PI, bcos, ALU.mult, ALU.add))
            p.op("vector", lambda e: e.tensor_scalar_min(bsin, bsin, 3.14159265))
            p.op("vector", lambda e: e.tensor_scalar_max(bsin, bsin, -3.14159265))
            p.op("vector", lambda e: e.tensor_scalar_min(bcos, bcos, 3.14159265))
            p.op("vector", lambda e: e.tensor_scalar_max(bcos, bcos, -3.14159265))
            p.op("scalar", lambda e: e.activation(bsin, bsin, AF.Sin),
                 waits=["ve"])
            p.op("scalar", lambda e: e.activation(bcos, bcos, AF.Sin))
            p.op("vector", lambda e, mmt=mmt: e.tensor_scalar(
                bscl, ptab[:, :], 1.0 / 128.0, -16.0, ALU.mult, ALU.add),
                waits=["se"])
            p.op("vector", lambda e: e.tensor_scalar_mul(bscl, bscl, slsc[:, 0:1]))
            p.op("scalar", lambda e: e.activation(bscl, bscl, AF.Exp),
                 waits=["ve"])
            p.op("vector", lambda e, t0=t0: e.tensor_tensor(
                tcs[:, t0:t0 + 512], bcos, bscl, ALU.mult), waits=["se"])
            p.op("vector", lambda e: e.tensor_tensor(btmp, bsin, bscl,
                                                     ALU.mult))
            p.op("vector", lambda e, t0=t0: e.tensor_scalar_mul(
                tss[:, t0:t0 + 512], btmp, ssgn[:, 0:1]))
            p.op("vector", lambda e: e.reciprocal(bscl, bscl))
            p.op("vector", lambda e, t0=t0: e.tensor_tensor(
                tci[:, t0:t0 + 512], bcos, bscl, ALU.mult))
            p.op("vector", lambda e: e.tensor_tensor(btmp, bsin, bscl,
                                                     ALU.mult))
            p.op("vector", lambda e, t0=t0: e.tensor_scalar_mul(
                tsi[:, t0:t0 + 512], btmp, ssgn[:, 0:1]))

        qmask = []
        r_ = np.arange(128)
        for wi, dt_ in enumerate((-256, -128, 0)):
            kq = (dt_ + r_[None, :]) - r_[:, None]
            m = ((kq <= 0) & (kq >= -256)).astype(np.float32)
            madd = ((1.0 - m) * -3.0e38).astype(ml_dtypes.bfloat16)
            mb = p.sbuf(f"qmaskb{wi}", [128, 128], BF16)
            qmask.append((mb, p.dma("sync", mb[:], nc.inline_tensor(madd, f"qmask{wi}").ap())))
        p.wait_only("vector", [t[1] for t in qmask])
        mall3 = p.sbuf("mall3", [128, 64], F32)
        mtr3 = p.sbuf("mtr3", [64, 128], F32)
        mstag3 = p.sbuf("mstag3", [1, 512], F32)
        lmask = []
        iq = np.arange(TB)
        for wi, dt_ in enumerate((-256, -128, 0, 128, 256, 384)):
            r = np.arange(128)
            kq = dt_ + r[:, None] - iq[None, :]
            m = ((kq <= 0) & (kq >= -256)).astype(np.float32)
            madd = ((1.0 - m) * -1.25e37).astype(ml_dtypes.bfloat16)
            mb = p.sbuf(f"lmaskb{wi}", [128, TB], BF16)
            lmask.append((mb, p.dma("sync", mb[:], nc.inline_tensor(madd, f"lmask{wi}").ap())))
        p.wait_only("vector", [t[1] for t in lmask])

        swqkv = p.sbuf("swqkv", [128, G, 3072], BF16)
        swo = p.sbuf("swo", [128, G, D], BF16)
        sqs = p.sbuf("sqs2", [128, 1], F32)
        sks = p.sbuf("sks2", [128, 1], F32)
        slng = p.sbuf("slng", [128, G], F32)
        slnb = p.sbuf("slnb", [128, G], F32)
        sb1a = p.sbuf("sb1a", [128, NAC], F32)
        sb1g = p.sbuf("sb1g", [128, NAC], F32)
        sb2 = p.sbuf("sb2c", [128, G], F32)
        xin = p.sbuf("xin3", [128, G, TB + 1], F32)
        xsh = p.sbuf("xsh", [128, G, TB], BF16)
        qr = p.sbuf("qr", [128, 8, TB], BF16)
        kr = p.sbuf("kr", [128, 8, TB], BF16)
        qh = xsh
        kh = [p.sbuf(f"kh{i}", [128, 8, TB], BF16) for i in range(2)]
        _khf = kh[0][:].rearrange("p g c -> p (g c)").bitcast(F32)
        bsin = _khf[:, 0:512]
        bcos = _khf[:, 512:1024]
        _khf2 = kh[1][:].rearrange("p g c -> p (g c)").bitcast(F32)
        bscl = _khf2[:, 0:512]
        btmp = _khf2[:, 512:1024]
        vtm = [p.sbuf(f"vtm{i}", [128, 4, NH, 65], BF16) for i in range(2)]
        for i in range(2):
            p.op("vector", lambda e, i=i: e.memset(
                vtm[i][:].rearrange("p a h c -> p (a h c)"), 1.0))
        sqt = p.sbuf("sqt3", [128, TB], F32)
        rrow = p.sbuf("rrow", [1, TB], F32)
        rotc = p.sbuf("rotc", [128, TB], BF16)
        qtmp = p.sbuf("qtmp", [128, TB], BF16)
        pexp = [p.sbuf(f"pexp3{i}", [128, TB], BF16) for i in range(2)]
        nbc = p.sbuf("nbc", [64, TB], F32)
        att = qr
        denr = p.sbuf("denr3", [1, TB], F32)
        hbuf = swqkv[:].rearrange("p g c -> p (g c)").rearrange(
            "p (a b) -> p a b", a=NAC + 26)[:, 0:NAC, :]
        w1a = [p.sbuf(f"w1a{i}", [128, G, 128], BF16) for i in range(2)]
        w1g = [p.sbuf(f"w1g{i}", [128, G, 128], BF16) for i in range(2)]
        w2c = [p.sbuf(f"w2c{i}", [128, D], BF16) for i in range(2)]
        xnh3 = kr
        identh3 = p.sbuf("identh3", [128, 128], BF16)
        identf3 = p.sbuf("identf3", [128, 128], F32)
        it3 = p.dma("sync", identh3[:], nc.inline_tensor(np.eye(128, dtype=ml_dtypes.bfloat16), "idh3").ap())
        it4 = p.dma("sync", identf3[:], nc.inline_tensor(np.eye(128, dtype=np.float32), "idf3").ap())
        p.wait_only("tensor", [it3, it4])
        p.shared_sq = sqt
        p.shared_pbc = pbig3[:, 3072:4096]
        _rowsv = swqkv[:].rearrange("p g c -> p (g c)").bitcast(F32)
        p.shared_rowbuf = _rowsv[0:1, G * TB:G * TB + 2 * TB]
        p.shared_r0 = _rowsv[0:1, G * TB + 2 * TB:G * TB + 3 * TB]
        p.shared_r1 = _rowsv[0:1, G * TB + 3 * TB:G * TB + 4 * TB]
        cm = make_common(p, TB, slim=True)

        def load_block(dst, src_ap, jb, extra=()):
            t0 = jb * TB
            w_ = ["ve", "se", "te"] + [e_ for e_ in extra if e_]
            if jb == 0:
                mz = p.op("vector", lambda e, dst=dst: e.memset(
                    dst[:, :, 0:1].rearrange("p g c -> p (g c)"), 0.0))
                return p.dma("sync", dst[:, :, 1:TB + 1], fm_src(src_ap, slice(0, TB)),
                             waits=w_ + [("ve", mz)])
            return p.dma("sync", dst[:, :, 0:TB + 1],
                         fm_src(src_ap, slice(t0 - 1, t0 + TB)), waits=w_)

        m3last = None
        st_tok = None
        for li in range(6):
            xsrc = x_in if li == 0 else xA
            wt1 = p.dma("gpsimd", swqkv[:], wqkv[li].rearrange("(g p) c -> p g c", p=128),
                        waits=["te"])
            wt2 = p.dma("gpsimd", swo[:], wo6[li].rearrange("(g p) c -> p g c", p=128),
                        waits=["te"])
            qts = [p.dma("sync", sqs[64 * i:64 * (i + 1), :],
                         qsc[li].rearrange("(a d) -> d a", a=1),
                         allow_slow_non_contiguous=True) for i in range(2)]
            kts = [p.dma("sync", sks[64 * i:64 * (i + 1), :],
                         ksc[li].rearrange("(a d) -> d a", a=1),
                         allow_slow_non_contiguous=True) for i in range(2)]
            p.wait_only("tensor", [wt1, wt2])
            p.wait_only("vector", qts + kts)
            for jb in range(NTB):
                t0 = jb * TB
                xld = load_block(xin, xsrc, jb, [st_tok])
                p.wait_only("vector", [xld])
                for g in range(G):
                    sl = (slice(1, TB + 1) if g < 4 else slice(0, TB))
                    p.op("vector", lambda e, g=g, sl=sl: e.tensor_copy(
                        xsh[:, g, :], xin[:, g, sl]))
                xsv = p.mark("ve")
                cur, prv = vtm[jb % 2], vtm[(jb + 1) % 2]
                kcur, kprv = kh[jb % 2], kh[(jb + 1) % 2]
                for og in range(24):
                    bank = ps2[og % 2]
                    mm = None
                    for kg in range(G):
                        mm = p.op("tensor", lambda e, og=og, kg=kg, bank=bank: e.matmul(
                            bank[:, 0:TB], swqkv[:, kg, og * 128:(og + 1) * 128],
                            xsh[:, kg, :], start=(kg == 0), stop=(kg == G - 1)),
                            waits=[("ve", xsv), "se", "ve"] if kg == 0 else [], inc="te")
                    if og < 16:
                        dst = qr if og < 8 else kr
                        p.op("vector", lambda e, og=og, bank=bank, dst=dst: e.tensor_copy(
                            dst[:, og % 8, :], bank[:, 0:TB]), waits=[("te", mm)])
                    else:
                        hp = og - 16
                        p.op("vector", lambda e, bank=bank: e.tensor_copy(
                            rotc[:, :], bank[:, 0:TB]), waits=[("te", mm)])
                        rv_ = p.mark("ve")
                        for ti in range(4):
                            tmt = p.op("tensor", lambda e, ti=ti: e.transpose(
                                ptab.bitcast(BF16)[:, 0:128],
                                rotc[:, ti * 128:(ti + 1) * 128], identh3[:]),
                                waits=[("ve", rv_), "ve"], inc="te")
                            for hh_ in range(2):
                                p.op("vector", lambda e, ti=ti, hp=hp, hh_=hh_, cur=cur: e.tensor_copy(
                                    cur[:, ti, 2 * hp + hh_, 0:64],
                                    ptab.bitcast(BF16)[:, hh_ * 64:hh_ * 64 + 64]),
                                    waits=[("te", tmt)] if hh_ == 0 else [])
                for hp in range(8):
                    for src_, dst_, scol, ctab, stab in ((qr, qh, sqs, tcs, tss),
                                                         (kr, kcur, sks, tci, tsi)):
                        sv = p.op("scalar", lambda e, hp=hp, src_=src_: e.activation(
                            sqt[:, :], src_[:, hp, :], AF.Square), waits=["ve", "te"])
                        for hh_ in range(2):
                            r0_ = 64 * hh_
                            mmn = p.op("tensor", lambda e, r0_=r0_: e.matmul(
                                pnr[0:1, 0:TB], onescol[r0_:r0_ + 64, 0:1],
                                sqt[r0_:r0_ + 64, :], start=True, stop=True),
                                waits=[("se", sv), "ve"], inc="te")
                            p.op("scalar", lambda e, mmn=mmn: e.activation(
                                rrow[0:1, :], pnr[0:1, 0:TB], AF.Sqrt),
                                waits=[("te", mmn)])
                            p.op("vector", lambda e: e.tensor_scalar_max(
                                rrow[0:1, :], rrow[0:1, :], 1e-12), waits=["se"])
                            p.op("vector", lambda e: e.reciprocal(rrow[0:1, :], rrow[0:1, :]))
                            rv_ = p.mark("ve")
                            bmm = p.op("tensor", lambda e: e.matmul(
                                pnr[0:64, 0:TB], p.ones_row[0:1, 0:64], rrow[0:1, 0:TB],
                                start=True, stop=True), waits=[("ve", rv_)], inc="te")
                            p.op("vector", lambda e, hp=hp, r0_=r0_, src_=src_, bmm=bmm: e.tensor_tensor(
                                qtmp[r0_:r0_ + 64, :], src_[r0_:r0_ + 64, hp, :],
                                pnr[0:64, 0:TB], ALU.mult), waits=[("te", bmm)])
                            p.op("vector", lambda e, r0_=r0_, scol=scol: e.tensor_scalar_mul(
                                qtmp[r0_:r0_ + 64, :], qtmp[r0_:r0_ + 64, :],
                                scol[r0_:r0_ + 64, 0:1]))
                        for a, b in ((0, 32), (32, 0), (64, 96), (96, 64)):
                            p.op("vector", lambda e, a=a, b=b: e.tensor_copy(
                                rotc[a:a + 32, :], qtmp[b:b + 32, :]))
                        p.op("vector", lambda e, hp=hp, t0=t0, ctab=ctab: e.tensor_tensor(
                            sqt.bitcast(BF16)[:, 0:TB], qtmp[:, :], ctab[:, t0:t0 + TB],
                            ALU.mult))
                        p.op("vector", lambda e, hp=hp, t0=t0, stab=stab: e.tensor_tensor(
                            rotc[:, :], rotc[:, :], stab[:, t0:t0 + TB], ALU.mult))
                        p.op("vector", lambda e, hp=hp, dst_=dst_: e.tensor_tensor(
                            dst_[:, hp, :],
                            sqt.bitcast(BF16)[:, 0:TB], rotc[:, :], ALU.add))
                if stop_after is not None and stop_after[0] == "v0" and stop_after[1] == li and jb == 0:
                    vv_ = p.mark("ve")
                    adtk = p.dma("gpsimd", x_out[0:128, 0:2080],
                                 cur[:, 0:2, :, :].rearrange("p a h c -> p (a h c)"),
                                 waits=[("ve", vv_), "te"])
                    p.wait_only("vector", [adtk])
                    p.wait_only("tensor", [adtk])
                if stop_after is not None and stop_after[0] in ("qh", "kh") and stop_after[1] == li:
                    dsrc = qh if stop_after[0] == "qh" else kcur
                    nv_ = p.mark("ve")
                    adtk = p.dma("gpsimd", fm_src(x_out, slice(t0, t0 + TB)),
                                 dsrc[:, :, :], waits=[("ve", nv_)])
                    p.wait_only("vector", [adtk])
                tiles = ([(prv, kprv, 2, -256), (prv, kprv, 3, -128)] if jb > 0 else []) + \
                        [(cur, kcur, 0, 0), (cur, kcur, 1, 128), (cur, kcur, 2, 256),
                         (cur, kcur, 3, 384)]
                for hp in range(8):
                    for hh_ in range(2):
                        h = 2 * hp + hh_
                        r0_ = 64 * hh_
                        for sub in range(4):
                            rel_tiles = []
                            for rel in (-2, -1, 0):
                                gt = 4 * jb + sub + rel
                                if gt < 0:
                                    continue
                                ksrc = kcur if gt // 4 == jb else kprv
                                rel_tiles.append((ksrc, gt % 4, rel))
                            nmm = None
                            for ri, (ksrc, ti, rel) in enumerate(rel_tiles):
                                nmm = p.op("tensor", lambda e, ksrc=ksrc, ti=ti, hp=hp, r0_=r0_, sub=sub, ri=ri: e.matmul(
                                    pnr[:, ri * 128:(ri + 1) * 128],
                                    qh[r0_:r0_ + 64, hp, sub * 128:(sub + 1) * 128],
                                    ksrc[r0_:r0_ + 64, hp, ti * 128:(ti + 1) * 128],
                                    start=True, stop=True),
                                    waits=["ve", "se"] if ri == 0 else [], inc="te")
                            for ri, (ksrc, ti, rel) in enumerate(rel_tiles):
                                p.op("vector", lambda e, ri=ri, rel=rel, nmm=nmm: e.tensor_tensor(
                                    pnr[:, ri * 128:(ri + 1) * 128],
                                    pnr[:, ri * 128:(ri + 1) * 128],
                                    qmask[rel + 2][0][:], ALU.add), waits=[("te", nmm)])
                            p.op("vector", lambda e, h=h, sub=sub, nr=len(rel_tiles): e.reduce_max(
                                mall3[:, h * 4 + sub:h * 4 + sub + 1],
                                pnr[:, 0:nr * 128], axis=AX.X))
                mv3 = p.mark("ve")
                if stop_after is not None and stop_after[0] == "m" and stop_after[1] == li:
                    adtk = p.dma("sync", x_out[0:128, jb * 64:(jb + 1) * 64],
                                 mall3[:, :], waits=[("ve", mv3)])
                    p.wait_only("vector", [adtk])
                tm3 = p.op("tensor", lambda e: e.transpose(
                    ptab[0:64, 0:128], mall3[:, :], identf3[:]),
                    waits=[("ve", mv3), "ve"], inc="te")
                p.op("vector", lambda e, tm3=tm3: e.tensor_scalar_mul(
                    mtr3[:], ptab[0:64, 0:128], -1.0), waits=[("te", tm3)])
                mcv3 = p.mark("ve")
                md1 = p.dma("sync", mhbm3, mtr3[:],
                            waits=[("ve", mcv3)] + ([m3last] if m3last else []))
                for hp in range(8):
                    for hh_ in range(2):
                        h = 2 * hp + hh_
                        r0_ = 64 * hh_
                        m3last = p.dma("gpsimd", mstag3[:],
                                       mhbm3.rearrange("(x a) c -> x (a c)", x=1)[0:1, h * 512:(h + 1) * 512],
                                       waits=[md1, "te"])
                        avl = None
                        nt = len(tiles)
                        for wi, (vsrc, ksrc, ti, dt_) in enumerate(tiles):
                            mi = (dt_ + 256) // 128
                            bank = ps2[wi % 2]
                            p.op("tensor", lambda e, ksrc=ksrc, ti=ti, hp=hp, r0_=r0_, bank=bank: e.matmul(
                                bank[:, 0:TB], ksrc[r0_:r0_ + 64, hp, ti * 128:(ti + 1) * 128],
                                qh[r0_:r0_ + 64, hp, :], start=True, stop=False),
                                waits=["ve", "se"], inc="te")
                            smm = p.op("tensor", lambda e, bank=bank: e.matmul(
                                bank[:, 0:TB], p.ones_row[0:1, :], mstag3[0:1, 0:512],
                                start=False, stop=True), waits=[m3last], inc="te")
                            p.op("vector", lambda e, bank=bank, mi=mi, smm=smm: e.tensor_tensor(
                                bank[:, 0:TB], bank[:, 0:TB], lmask[mi][0][:], ALU.add),
                                waits=[("te", smm)])
                            ex = p.op("scalar", lambda e, bank=bank, wi=wi: e.activation(
                                pexp[wi % 2][:], bank[:, 0:TB], AF.Exp, scale=QKS),
                                waits=["ve"])
                            avl = p.op("tensor", lambda e, vsrc=vsrc, ti=ti, h=h, hh_=hh_, wi=wi, nt=nt: e.matmul(
                                pav[hh_][:, 0:TB], vsrc[:, ti, h, :], pexp[wi % 2][:],
                                start=(wi == 0), stop=(wi == nt - 1)),
                                waits=[("se", ex)] + (["ve"] if wi == 0 else []), inc="te")
                        p.op("vector", lambda e, hh_=hh_, avl=avl: e.reciprocal(
                            denr[0:1, :], pav[hh_][64:65, 0:TB]), waits=[("te", avl)])
                        rv_ = p.mark("ve")
                        dmm = p.op("tensor", lambda e: e.matmul(
                            pnr[0:64, 0:TB], p.ones_row[0:1, 0:64], denr[0:1, 0:TB],
                            start=True, stop=True), waits=[("ve", rv_)], inc="te")
                        p.op("vector", lambda e, dmm=dmm: e.tensor_copy(
                            nbc[:, :], pnr[0:64, 0:TB]), waits=[("te", dmm)])
                        if stop_after is not None and stop_after[0] == "den":
                            p.op("vector", lambda e, hp=hp, r0_=r0_: e.tensor_copy(
                                att[r0_:r0_ + 64, hp, :], nbc[:, :]))
                        else:
                            p.op("vector", lambda e, h=h, hh_=hh_, r0_=r0_, hp=hp: e.tensor_tensor(
                                att[r0_:r0_ + 64, hp, :], pav[hh_][0:64, 0:TB], nbc[:, :],
                                ALU.mult))
                atv = p.mark("ve")
                if stop_after is not None and stop_after[0] in ("att", "den") and stop_after[1] == li:
                    adtk = p.dma("gpsimd", fm_src(x_out, slice(t0, t0 + TB)),
                                 att[:, :, :], waits=[("ve", atv)])
                    p.wait_only("vector", [adtk])
                for og in range(G):
                    bank = ps2[og % 2]
                    mm = None
                    for kg in range(G):
                        mm = p.op("tensor", lambda e, og=og, kg=kg, bank=bank: e.matmul(
                            bank[:, 0:TB], swo[:, kg, og * 128:(og + 1) * 128],
                            att[:, kg, :], start=(kg == 0), stop=(kg == G - 1)),
                            waits=[("ve", atv), "se", "ve"] if kg == 0 else [], inc="te")
                    p.op("vector", lambda e, og=og, bank=bank, mm=mm: e.tensor_add(
                        xin[:, og, 1:TB + 1], bank[:, 0:TB], xin[:, og, 1:TB + 1]),
                        waits=[("te", mm)])
                st_tok = p.dma("sync", fm_src(xB, slice(t0, t0 + TB)),
                               xin[:, :, 1:TB + 1], waits=["ve"])
            if stop_after is not None and stop_after[0] in ("att", "qh", "kh", "m", "den", "v0") and stop_after[1] == li:
                for qn in list(p.dqueue):
                    p.dma_flush(qn)
                p.finish()
                return nc
            if stop_after == ("A", li):
                dtk = p.dma("sync", x_out, xB, waits=[st_tok])
                p.dma_flush("sync")
                for qn in list(p.dqueue):
                    p.dma_flush(qn)
                p.finish()
                return nc
            g1 = p.dma("sync", slng[:, :], lng[li].rearrange("(g p) -> p g", p=128),
                       allow_slow_non_contiguous=True)
            g2 = p.dma("sync", slnb[:, :], lnb[li].rearrange("(g p) -> p g", p=128),
                       allow_slow_non_contiguous=True)
            g3 = p.dma("sync", sb1a[:, :], b16[li, 0:2816].rearrange("(g p) -> p g", p=128),
                       allow_slow_non_contiguous=True)
            g4 = p.dma("sync", sb1g[:, 0:21], b16[li, 2730:5418].rearrange("(g p) -> p g", p=128),
                       allow_slow_non_contiguous=True)
            g5 = p.dma("sync", sb1g[0:42, 21:22], b16[li, 5418:5460].rearrange("(g p) -> p g", p=42)
                       .rearrange("p g -> p (g)").rearrange("p (g) -> p g", g=1),
                       allow_slow_non_contiguous=True)
            g6 = p.dma("sync", sb2[:, :], b26[li].rearrange("(g p) -> p g", p=128),
                       allow_slow_non_contiguous=True)
            p.wait_only("vector", [g1, g2, g3, g4, g5, g6])
            p.wait_only("scalar", [g1, g2, g3, g4, g5, g6])
            for jb in range(NTB):
                t0 = jb * TB
                xld = load_block(xin, xB, jb, [st_tok])
                p.wait_only("vector", [xld])
                for g in range(G):
                    sl = (slice(1, TB + 1) if g < 4 else slice(0, TB))
                    p.op("vector", lambda e, g=g, sl=sl: e.tensor_copy(
                        sqt.bitcast(BF16)[:, 0:TB], xin[:, g, sl])
                        if False else e.tensor_copy(xnh3[:, g, :], xin[:, g, sl]))
                xsv = p.mark("ve")
                ln_in = xnh3
                emit_ln_fm(p, ln_in, ln_in, cm, TB)
                for g in range(G):
                    p.op("vector", lambda e, g=g: e.tensor_scalar_mul(
                        ln_in[:, g, 0:TB], ln_in[:, g, 0:TB], slng[:, g:g + 1]))
                    p.op("vector", lambda e, g=g: e.tensor_scalar_add(
                        ln_in[:, g, 0:TB], ln_in[:, g, 0:TB], slnb[:, g:g + 1]))
                lnv = p.mark("ve")
                wtoks = [None, None]
                for j in range(NAC):
                    aw = min(128, DI - 128 * j)
                    buf_a, buf_g = w1a[j % 2], w1g[j % 2]
                    wtoks[j % 2] = p.dma("gpsimd", buf_a[:, :, 0:aw],
                                         w16[li, :, 128 * j:128 * j + aw]
                                         .rearrange("(g p) c -> p g c", p=128),
                                         waits=["te"])
                    tk2 = p.dma("gpsimd", buf_g[:, :, 0:aw],
                                w16[li, :, DI + 128 * j:DI + 128 * j + aw]
                                .rearrange("(g p) c -> p g c", p=128), waits=["te"])
                    m1 = m2 = None
                    for kg in range(G):
                        m1 = p.op("tensor", lambda e, kg=kg, j=j, aw=aw, buf_a=buf_a: e.matmul(
                            ps2[0][0:aw, 0:TB], buf_a[:, kg, 0:aw], ln_in[:, kg, :],
                            start=(kg == 0), stop=(kg == G - 1)),
                            waits=[wtoks[j % 2], tk2, ("ve", lnv), "se", "ve"] if kg == 0 else [],
                            inc="te")
                    for kg in range(G):
                        m2 = p.op("tensor", lambda e, kg=kg, j=j, aw=aw, buf_g=buf_g: e.matmul(
                            ps2[1][0:aw, 0:TB], buf_g[:, kg, 0:aw], ln_in[:, kg, :],
                            start=(kg == 0), stop=(kg == G - 1)), waits=[], inc="te")
                    p.op("scalar", lambda e, j=j, aw=aw, m2=m2: e.activation(
                        bsin[0:aw, 0:TB], ps2[0][0:aw, 0:TB], AF.Identity,
                        bias=sb1a[0:aw, j:j + 1]), waits=[("te", m2), "ve"])
                    p.op("scalar", lambda e, j=j, aw=aw: e.activation(
                        bcos[0:aw, 0:TB], ps2[1][0:aw, 0:TB], AF.Gelu,
                        bias=sb1g[0:aw, j:j + 1]))
                    p.op("vector", lambda e, j=j, aw=aw: e.tensor_tensor(
                        hbuf[0:aw, j, :], bsin[0:aw, 0:TB], bcos[0:aw, 0:TB], ALU.mult),
                        waits=["se"])
                hv = p.mark("ve")
                w2toks = [None, None]
                lastmm = None
                for kc in range(NAC):
                    kw = min(128, DI - 128 * kc)
                    buf = w2c[kc % 2]
                    w2toks[kc % 2] = p.dma("gpsimd", buf[0:kw, :],
                                           w26[li, 128 * kc:128 * kc + kw, :],
                                           waits=["te"])
                    for og in range(G):
                        lastmm = p.op("tensor", lambda e, kc=kc, og=og, kw=kw, buf=buf: e.matmul(
                            pfb[og][:, 0:TB], buf[0:kw, og * 128:(og + 1) * 128],
                            hbuf[0:kw, kc, :], start=(kc == 0), stop=(kc == NAC - 1)),
                            waits=[w2toks[kc % 2], ("ve", hv), "se"] if og == 0 else [],
                            inc="te")
                for og in range(G):
                    p.op("scalar", lambda e, og=og, lastmm=lastmm: e.activation(
                        sqt[:, 0:TB], pfb[og][:, 0:TB], AF.Identity,
                        bias=sb2[:, og:og + 1]), waits=[("te", lastmm), "ve"])
                    p.op("vector", lambda e, og=og: e.tensor_add(
                        xin[:, og, 1:TB + 1], sqt[:, 0:TB], xin[:, og, 1:TB + 1]),
                        waits=["se"])
                st_tok = p.dma("sync", fm_src(xA, slice(t0, t0 + TB)),
                               xin[:, :, 1:TB + 1], waits=["ve"])
            if stop_after == ("B", li):
                dtk = p.dma("sync", x_out, xA, waits=[st_tok])
                for qn in list(p.dqueue):
                    p.dma_flush(qn)
                p.finish()
                return nc
        xf2 = swqkv[:].rearrange("p g c -> p (g c)").bitcast(F32)[:, 0:G * TB] \
            .rearrange("p (g c) -> p g c", g=G)
        for jb in range(NTB):
            t0 = jb * TB
            x1 = p.dma("sync", xin[:, :, 0:TB], fm_src(xA, slice(t0, t0 + TB)),
                       waits=(["ve", "se", "te"] + ([st_tok] if st_tok else [])))
            x2 = p.dma("sync", xf2, fm_src(x_in, slice(t0, t0 + TB)),
                       waits=(["ve", "se", "te"] + ([st_tok] if st_tok else [])))
            p.wait_only("vector", [x1, x2])
            for g in range(G):
                p.op("vector", lambda e, g=g: e.tensor_add(
                    xf2[:, g, :], xf2[:, g, :], xin[:, g, 0:TB]))
            emit_ln_fm(p, xf2, xf2, cm, TB)
            st_tok = p.dma("sync", fm_src(x_out, slice(t0, t0 + TB)), xf2,
                           waits=["ve"])
        for qn in list(p.dqueue):
            p.dma_flush(qn)
        p.finish()
    return nc


MOE_N = NCORE * T
MOE_CAP = 5120
MOE_E = 8
MOE_F = 4096


def build_l0_logits():
    nc = bass.Bass("TRN2", target_bir_lowering=False)
    x_in = nc.dram_tensor("x_in", [D, T], F32, kind="ExternalInput").ap()
    rw = nc.dram_tensor("rw", [MOE_E, D], F32, kind="ExternalInput").ap()
    lg_out = nc.dram_tensor("lg_out", [T, MOE_E], F32, kind="ExternalOutput").ap()
    with contextlib.ExitStack() as ctx:
        p = P(nc, ctx)
        rwsb = p.sbuf("rwsb", [128, G, MOE_E], F32)
        xsb = [p.sbuf(f"xlg{i}", [128, G, TB], F32) for i in range(2)]
        lrow = p.sbuf("lrow", [128, MOE_E], F32)
        plg = p.psum("plg", [128, 512], F32)
        tws = [p.dma("sync", rwsb[:, kg, :],
                     rw.rearrange("e (g p) -> p g e", p=128)[:, kg, :],
                     allow_slow_non_contiguous=True) for kg in range(G)]
        p.wait_only("tensor", tws)
        st = None
        for jb in range(NTB):
            xb = xsb[jb % 2]
            xt = p.dma("sync", xb[:], fm_src(x_in, slice(jb * TB, (jb + 1) * TB)),
                       waits=["te"])
            for c4 in range(4):
                mm = None
                for kg in range(G):
                    mm = p.op("tensor", lambda e, xb=xb, kg=kg, c4=c4: e.matmul(
                        plg[:, 0:MOE_E], xb[:, kg, c4 * 128:(c4 + 1) * 128],
                        rwsb[:, kg, :], start=(kg == 0), stop=(kg == G - 1)),
                        waits=[xt, "ve"] if kg == 0 else [], inc="te")
                p.op("vector", lambda e, mm=mm: e.tensor_copy(lrow[:], plg[:, 0:MOE_E]),
                     waits=[("te", mm)] + ([st] if st else []))
                lv = p.mark("ve")
                st = p.dma("sync", lg_out[jb * TB + c4 * 128:jb * TB + (c4 + 1) * 128, :],
                           lrow[:], waits=[("ve", lv)])
        for qn in list(p.dqueue):
            p.dma_flush(qn)
        p.finish()
    return nc


def build_l0_moe():
    NCH = MOE_N // 128
    NTC = MOE_N // 512
    nc = bass.Bass("TRN2", target_bir_lowering=False)
    lgall = nc.dram_tensor("lgall", [MOE_N, MOE_E], F32, kind="ExternalInput").ap()
    xfm = nc.dram_tensor("xfm", [D, MOE_N], F32, kind="ExternalInput").ap()
    eid = nc.dram_tensor("eid", [128, 1], F32, kind="ExternalInput").ap()
    wg_e = nc.dram_tensor("wg_e", [D, MOE_F], F32, kind="ExternalInput").ap()
    wu_e = nc.dram_tensor("wu_e", [D, MOE_F], F32, kind="ExternalInput").ap()
    wd_e = nc.dram_tensor("wd_e", [MOE_F, D], F32, kind="ExternalInput").ap()
    eo_e = nc.dram_tensor("eo_e", [D, MOE_N], BF16, kind="ExternalOutput").ap()
    gwdr = nc.dram_tensor("gwdr", [NCH, 128], F32, kind="Internal").ap()
    with contextlib.ExitStack() as ctx:
        p = P(nc, ctx)
        wgres = p.sbuf("wgres", [128, G, MOE_F], BF16)
        wures = p.sbuf("wures", [128, G, MOE_F], BF16)
        _wgf = wgres[:].rearrange("p g f -> p (g f)").bitcast(F32)
        _wuf = wures[:].rearrange("p g f -> p (g f)").bitcast(F32)
        lg = _wgf[:, 0:2048].rearrange("p (c e) -> p c e", e=MOE_E)
        prb = _wgf[:, 2048:4096].rearrange("p (c e) -> p c e", e=MOE_E)
        names = ["r1", "r2", "ii1", "ii2", "fw1", "fw2", "tha", "thb",
                 "wrk", "wrk2", "me1f", "me2f", "gwtok"]
        sl = {}
        for i, nm in enumerate(names):
            sl[nm] = _wuf[:, i * NCH:(i + 1) * NCH]
        r1, r2, ii1, ii2 = sl["r1"], sl["r2"], sl["ii1"], sl["ii2"]
        fw1, fw2, tha, thb = sl["fw1"], sl["fw2"], sl["tha"], sl["thb"]
        wrk, wrk2, me1f, me2f = sl["wrk"], sl["wrk2"], sl["me1f"], sl["me2f"]
        gwtok = sl["gwtok"]
        eidc = p.sbuf("eidc", [128, 1], F32)
        rowb = p.sbuf("rowb", [1, 64], F32)
        LO, HI, MD, CT, GE, TTs, CPr = 0, 8, 16, 24, 32, 40, 48
        onesrow1 = p.sbuf("onesrow1", [1, 128], F32)
        onescolb = p.sbuf("onescolbM", [128, 1], BF16)
        identf = p.sbuf("identfM", [128, 128], F32)
        cntb = p.sbuf("cntb", [128, NCH], BF16)
        gtr = p.sbuf("gtr", [128, 128], F32)
        gwrow = p.sbuf("gwrow", [1, 512], F32)
        gbs = p.sbuf("gbs", [128, 512], F32)
        hh = p.sbuf("hh", [128, 32, 512], BF16)
        Xd = p.sbuf("Xd", [128, G, 512], BF16)
        eofm = p.sbuf("eofm", [128, G, 512], BF16)
        sgb = p.sbuf("sgb", [128, 512], BF16)
        wdc = p.sbuf("wdc", [128, 32, 128], BF16)
        pA = p.psum("pAm", [128, 512], F32)
        pB = p.psum("pBm", [128, 512], F32)
        pC = p.psum("pCm", [128, 512], F32)
        pD = p.psum("pDm", [128, 512], F32)

        tlg = p.dma("sync", lg, lgall.rearrange("(c p) e -> p c e", p=128))
        tei = p.dma("sync", eidc[:], eid)
        tif = p.dma("sync", identf[:],
                    nc.inline_tensor(np.eye(128, dtype=np.float32), "idfM").ap())
        p.op("vector", lambda e: e.memset(onesrow1[:], 1.0))
        p.op("vector", lambda e: e.memset(onescolb[:], 1.0))
        p.wait_only("vector", [tlg, tei])
        p.wait_only("tensor", [tif])

        p.op("vector", lambda e: e.tensor_copy(r1, lg[:, :, 0]))
        for e_ in range(1, MOE_E):
            p.op("vector", lambda e, e_=e_: e.tensor_tensor(
                r1, r1, lg[:, :, e_], ALU.max))
        for e_ in range(MOE_E):
            p.op("vector", lambda e, e_=e_: e.tensor_tensor(
                wrk, lg[:, :, e_], r1, ALU.subtract))
            p.op("scalar", lambda e: e.activation(wrk, wrk, AF.Exp), waits=["ve"])
            p.op("vector", lambda e, e_=e_: e.tensor_copy(prb[:, :, e_], wrk),
                 waits=["se"])
        p.op("vector", lambda e: e.tensor_copy(r2, prb[:, :, 0]))
        for e_ in range(1, MOE_E):
            p.op("vector", lambda e, e_=e_: e.tensor_tensor(
                r2, r2, prb[:, :, e_], ALU.add))
        p.op("vector", lambda e: e.reciprocal(r2, r2))
        for e_ in range(MOE_E):
            p.op("vector", lambda e, e_=e_: e.tensor_tensor(
                prb[:, :, e_], prb[:, :, e_], r2, ALU.mult))
        p.op("vector", lambda e: e.tensor_copy(r1, prb[:, :, 0]))
        for e_ in range(1, MOE_E):
            p.op("vector", lambda e, e_=e_: e.tensor_tensor(
                r1, r1, prb[:, :, e_], ALU.max))
        p.op("vector", lambda e: e.memset(ii1, 0.0))
        for e_ in range(MOE_E):
            p.op("vector", lambda e, e_=e_: e.tensor_tensor(
                wrk, prb[:, :, e_], r1, ALU.is_equal))
            if e_ > 0:
                p.op("vector", lambda e, e_=e_: e.scalar_tensor_tensor(
                    ii1, wrk, float(e_), ii1, ALU.mult, ALU.add))
            p.op("vector", lambda e, e_=e_: e.scalar_tensor_tensor(
                prb[:, :, e_], wrk, -2.0, prb[:, :, e_], ALU.mult, ALU.add))
        p.op("vector", lambda e: e.tensor_copy(r2, prb[:, :, 0]))
        for e_ in range(1, MOE_E):
            p.op("vector", lambda e, e_=e_: e.tensor_tensor(
                r2, r2, prb[:, :, e_], ALU.max))
        p.op("vector", lambda e: e.memset(ii2, 0.0))
        for e_ in range(1, MOE_E):
            p.op("vector", lambda e, e_=e_: e.tensor_tensor(
                wrk, prb[:, :, e_], r2, ALU.is_equal))
            p.op("vector", lambda e, e_=e_: e.scalar_tensor_tensor(
                ii2, wrk, float(e_), ii2, ALU.mult, ALU.add))
        p.op("vector", lambda e: e.tensor_tensor(wrk, r1, r2, ALU.add))
        p.op("vector", lambda e: e.tensor_scalar_add(wrk, wrk, 1e-6))
        p.op("vector", lambda e: e.reciprocal(wrk, wrk))
        p.op("vector", lambda e: e.tensor_tensor(fw1, r1, wrk, ALU.mult))
        p.op("vector", lambda e: e.tensor_tensor(fw2, r2, wrk, ALU.mult))
        p.op("vector", lambda e: e.memset(rowb[0:1, LO:LO + 8], 0.0))
        p.op("vector", lambda e: e.memset(rowb[0:1, HI:HI + 8], 1.0))
        p.op("vector", lambda e: e.memset(rowb[0:1, CPr:CPr + 8], float(MOE_CAP)))
        for it in range(28):
            p.op("vector", lambda e: e.tensor_tensor(
                rowb[0:1, MD:MD + 8], rowb[0:1, LO:LO + 8], rowb[0:1, HI:HI + 8],
                ALU.add))
            p.op("vector", lambda e: e.tensor_scalar_mul(
                rowb[0:1, MD:MD + 8], rowb[0:1, MD:MD + 8], 0.5))
            mv = p.mark("ve")
            mmid = p.op("tensor", lambda e: e.matmul(
                pA[:, 0:8], onesrow1[0:1, :], rowb[0:1, MD:MD + 8],
                start=True, stop=True), waits=[("ve", mv), "ve"], inc="te")
            for e_ in range(MOE_E):
                p.op("vector", lambda e, e_=e_, mmid=mmid: e.tensor_single_scalar(
                    tha, fw1, pA[:, e_:e_ + 1], ALU.is_gt), waits=[("te", mmid)])
                p.op("vector", lambda e, e_=e_: e.tensor_single_scalar(
                    thb, ii1, float(e_), ALU.is_equal))
                p.op("vector", lambda e: e.tensor_tensor(tha, tha, thb, ALU.mult))
                p.op("vector", lambda e, e_=e_: e.tensor_single_scalar(
                    thb, fw2, pA[:, e_:e_ + 1], ALU.is_gt))
                p.op("vector", lambda e, e_=e_: e.tensor_single_scalar(
                    wrk, ii2, float(e_), ALU.is_equal))
                p.op("vector", lambda e: e.tensor_tensor(thb, thb, wrk, ALU.mult))
                p.op("vector", lambda e: e.tensor_tensor(tha, tha, thb, ALU.add))
                p.op("vector", lambda e: e.tensor_copy(cntb[:], tha))
                bv = p.mark("ve")
                cmm = p.op("tensor", lambda e: e.matmul(
                    pB[0:1, 0:NCH], onescolb[:, 0:1], cntb[:],
                    start=True, stop=True), waits=[("ve", bv), "ve"], inc="te")
                p.op("vector", lambda e, e_=e_, cmm=cmm: e.reduce_sum(
                    rowb[0:1, CT + e_:CT + e_ + 1], pB[0:1, 0:NCH], axis=AX.X),
                    waits=[("te", cmm)])
            p.op("vector", lambda e: e.tensor_tensor(
                rowb[0:1, GE:GE + 8], rowb[0:1, CT:CT + 8], rowb[0:1, CPr:CPr + 8],
                ALU.is_ge))
            p.op("vector", lambda e: e.tensor_tensor(
                rowb[0:1, TTs:TTs + 8], rowb[0:1, MD:MD + 8], rowb[0:1, LO:LO + 8],
                ALU.subtract))
            p.op("vector", lambda e: e.tensor_tensor(
                rowb[0:1, TTs:TTs + 8], rowb[0:1, TTs:TTs + 8], rowb[0:1, GE:GE + 8],
                ALU.mult))
            p.op("vector", lambda e: e.tensor_tensor(
                rowb[0:1, LO:LO + 8], rowb[0:1, LO:LO + 8], rowb[0:1, TTs:TTs + 8],
                ALU.add))
            p.op("vector", lambda e: e.tensor_tensor(
                rowb[0:1, TTs:TTs + 8], rowb[0:1, HI:HI + 8], rowb[0:1, MD:MD + 8],
                ALU.subtract))
            p.op("vector", lambda e: e.tensor_tensor(
                rowb[0:1, TTs:TTs + 8], rowb[0:1, TTs:TTs + 8], rowb[0:1, GE:GE + 8],
                ALU.mult))
            p.op("vector", lambda e: e.tensor_tensor(
                rowb[0:1, HI:HI + 8], rowb[0:1, MD:MD + 8], rowb[0:1, TTs:TTs + 8],
                ALU.add))
        lv = p.mark("ve")
        mlo = p.op("tensor", lambda e: e.matmul(
            pA[:, 0:8], onesrow1[0:1, :], rowb[0:1, LO:LO + 8], start=True, stop=True),
            waits=[("ve", lv), "ve"], inc="te")
        p.op("vector", lambda e: e.memset(tha, 0.0))
        p.op("vector", lambda e: e.memset(thb, 0.0))
        for e_ in range(MOE_E):
            p.op("vector", lambda e, e_=e_, mlo=mlo: e.tensor_single_scalar(
                wrk, ii1, float(e_), ALU.is_equal), waits=[("te", mlo)])
            p.op("vector", lambda e, e_=e_: e.tensor_single_scalar(
                wrk, wrk, pA[:, e_:e_ + 1], ALU.mult))
            p.op("vector", lambda e: e.tensor_tensor(tha, tha, wrk, ALU.add))
            p.op("vector", lambda e, e_=e_: e.tensor_single_scalar(
                wrk, ii2, float(e_), ALU.is_equal))
            p.op("vector", lambda e, e_=e_: e.tensor_single_scalar(
                wrk, wrk, pA[:, e_:e_ + 1], ALU.mult))
            p.op("vector", lambda e: e.tensor_tensor(thb, thb, wrk, ALU.add))
        p.op("vector", lambda e: e.tensor_tensor(me1f, fw1, tha, ALU.is_gt))
        p.op("vector", lambda e: e.tensor_single_scalar(
            wrk, ii1, eidc[:, 0:1], ALU.is_equal))
        p.op("vector", lambda e: e.tensor_tensor(me1f, me1f, wrk, ALU.mult))
        p.op("vector", lambda e: e.tensor_tensor(me2f, fw2, thb, ALU.is_gt))
        p.op("vector", lambda e: e.tensor_single_scalar(
            wrk, ii2, eidc[:, 0:1], ALU.is_equal))
        p.op("vector", lambda e: e.tensor_tensor(me2f, me2f, wrk, ALU.mult))
        p.op("vector", lambda e: e.tensor_tensor(gwtok, me1f, fw1, ALU.mult))
        p.op("vector", lambda e: e.tensor_tensor(wrk, me2f, fw2, ALU.mult))
        p.op("vector", lambda e: e.tensor_tensor(gwtok, gwtok, wrk, ALU.add))
        gv = p.mark("ve")
        gts = []
        for half in range(2):
            tmt = p.op("tensor", lambda e, half=half: e.transpose(
                pD[:, 0:128], gwtok.rearrange(
                    "p (a c) -> p a c", a=2)[:, half, :], identf[:]),
                waits=[("ve", gv), "ve"], inc="te")
            p.op("vector", lambda e, tmt=tmt: e.tensor_copy(gtr[:], pD[:, 0:128]),
                 waits=[("te", tmt)] + ([gts[-1]] if gts else []))
            cv = p.mark("ve")
            gts.append(p.dma("sync", gwdr[half * 128:(half + 1) * 128, :], gtr[:],
                             waits=[("ve", cv)]))
        rv = p.mark("ve")
        twg = p.dma("gpsimd", wgres[:], wg_e.rearrange("(g p) f -> p g f", p=128),
                    waits=[("ve", rv), gts[-1]])
        twu = p.dma("gpsimd", wures[:], wu_e.rearrange("(g p) f -> p g f", p=128),
                    waits=[("ve", rv), gts[-1]])
        p.wait_only("tensor", [twg, twu])
        p.wait_only("vector", [twg, twu])
        eotok = None
        wdtok = None
        for tc in range(NTC):
            t0 = tc * 512
            xt = p.dma("gpsimd", Xd[:],
                       fm_src(xfm, slice(t0, t0 + 512)), waits=["te"])
            gt = p.dma("sync", gwrow[:],
                       gwdr.rearrange("c p -> (c p)")[t0:t0 + 512]
                       .rearrange("(x t) -> x t", x=1), waits=["ve"])
            gmm = p.op("tensor", lambda e: e.matmul(
                pD[:, 0:512], onesrow1[0:1, :], gwrow[0:1, :],
                start=True, stop=True), waits=[gt, "ve"], inc="te")
            p.op("vector", lambda e, gmm=gmm: e.tensor_copy(gbs[:], pD[:, 0:512]),
                 waits=[("te", gmm)])
            for fc in range(32):
                mmg = mmu = None
                for kg in range(G):
                    mmg = p.op("tensor", lambda e, kg=kg, fc=fc: e.matmul(
                        pA[:, 0:512], wgres[:, kg, fc * 128:(fc + 1) * 128],
                        Xd[:, kg, :], start=(kg == 0), stop=(kg == G - 1)),
                        waits=[xt, "ve", "se"] if kg == 0 else [], inc="te")
                for kg in range(G):
                    mmu = p.op("tensor", lambda e, kg=kg, fc=fc: e.matmul(
                        pB[:, 0:512], wures[:, kg, fc * 128:(fc + 1) * 128],
                        Xd[:, kg, :], start=(kg == 0), stop=(kg == G - 1)),
                        inc="te")
                sg_ = p.op("scalar", lambda e, mmg=mmg: e.activation(
                    sgb[:], pA[:, 0:512], AF.Silu), waits=[("te", mmg), "ve"])
                p.op("vector", lambda e, fc=fc, mmu=mmu, sg_=sg_: e.tensor_tensor(
                    hh[:, fc, :], sgb[:], pB[:, 0:512], ALU.mult),
                    waits=[("te", mmu), ("se", sg_)])
            hv = p.mark("ve")
            for og in range(G):
                wdtok = p.dma("gpsimd", wdc[:],
                              wd_e.rearrange("(f p) d -> p f d", p=128)[:, :, og * 128:(og + 1) * 128],
                              waits=["te"])
                mm = None
                for fc in range(32):
                    mm = p.op("tensor", lambda e, fc=fc: e.matmul(
                        pC[:, 0:512], wdc[:, fc, :], hh[:, fc, :],
                        start=(fc == 0), stop=(fc == 31)),
                        waits=[wdtok, ("ve", hv), "ve"] if fc == 0 else [], inc="te")
                p.op("vector", lambda e, og=og, mm=mm: e.tensor_tensor(
                    eofm[:, og, :], pC[:, 0:512], gbs[:], ALU.mult),
                    waits=[("te", mm)] + ([eotok] if eotok and og == 0 else []))
            ev = p.mark("ve")
            eotok = p.dma("sync", fm_src(eo_e, slice(t0, t0 + 512)), eofm[:, :, :],
                          waits=[("ve", ev)])
        for qn in list(p.dqueue):
            p.dma_flush(qn)
        p.finish()
    return nc


def build_l0_moefin():
    nc = bass.Bass("TRN2", target_bir_lowering=False)
    x1f = nc.dram_tensor("x1f", [D, T], F32, kind="ExternalInput").ap()
    eos = [nc.dram_tensor(f"eo{e}", [D, T], BF16, kind="ExternalInput").ap()
           for e in range(MOE_E)]
    x_out = nc.dram_tensor("x_out", [D, T], F32, kind="ExternalOutput").ap()
    with contextlib.ExitStack() as ctx:
        p = P(nc, ctx)
        xb = p.sbuf("xb", [128, G, TB], F32)
        eb = [p.sbuf(f"ebf{i}", [128, G, TB], F32) for i in range(2)]
        cm = make_common(p, TB)
        st = None
        for jb in range(NTB):
            t0 = jb * TB
            x1 = p.dma("sync", xb[:], fm_src(x1f, slice(t0, t0 + TB)),
                       waits=["ve"] + ([st] if st else []))
            p.wait_only("vector", [x1])
            etoks = {}
            for e_ in range(MOE_E):
                r = e_ % 2
                w0 = [etoks[e_ - 2]] if e_ >= 2 else []
                etoks[e_] = p.dma("gpsimd", eb[r][:],
                                  fm_src(eos[e_], slice(t0, t0 + TB)),
                                  waits=w0 + ["ve"])
                for g in range(G):
                    p.op("vector", lambda e, g=g, r=r: e.tensor_add(
                        xb[:, g, :], xb[:, g, :], eb[r][:, g, :]),
                        waits=[etoks[e_]] if g == 0 else [])
            emit_ln_fm(p, xb, xb, cm, TB)
            lv = p.mark("ve")
            st = p.dma("sync", fm_src(x_out, slice(t0, t0 + TB)), xb[:, :, :],
                       waits=[("ve", lv)])
        for qn in list(p.dqueue):
            p.dma_flush(qn)
        p.finish()
    return nc
